# version 03
# baseline (speedup 1.0000x reference)
import math
import numpy as np
import concourse.bass as bass
import concourse.mybir as mybir
from concourse.bass_utils import run_bass_kernel_spmd
from contextlib import ExitStack

F32 = mybir.dt.float32
BF16 = mybir.dt.bfloat16
AF = mybir.ActivationFunctionType
ALU = mybir.AluOpType
AX = mybir.AxisListType

NT = 2048
NTC = 16
H = 1024
NK = 8
NE = 16
IM = 512
NHEAD = 16
HD = 64
STAGE = 3
DBG = 0


class Sched:

    def __init__(self):
        self.ops = []
        self.last_w = {}
        self.readers = {}
        self.drain_mode = False
        self._last_was_drain = {}

    def add(self, eng, fn, reads=(), writes=(), dma=None):
        if (self.drain_mode and fn is not None and dma is None
                and eng in ("vector", "scalar")
                and not self._last_was_drain.get(eng, False)):
            self._last_was_drain[eng] = True
            dr = lambda h: h.drain()
            dr._is_drain = True
            self.add(eng, dr)
        self._last_was_drain[eng] = (fn is not None and dma is None
                                     and getattr(fn, "_is_drain", False))
        idx = len(self.ops)
        deps = set()
        for t in reads:
            if t in self.last_w:
                deps.add(self.last_w[t])
        for t in writes:
            if t in self.last_w:
                deps.add(self.last_w[t])
            for r in self.readers.get(t, ()):
                deps.add(r)
        self.ops.append(dict(idx=idx, eng=eng, fn=fn, deps=deps, dma=dma,
                             inc=None, dma_idx=None, waits=None))
        for t in reads:
            self.readers.setdefault(t, []).append(idx)
        for t in writes:
            self.last_w[t] = idx
            self.readers[t] = []
        return idx

    def emit(self, nc, ctx):
        ops = self.ops
        qcnt = {}
        for op in ops:
            if op["dma"]:
                q = op["dma"]
                op["dma_idx"] = qcnt.get(q, 0)
                qcnt[q] = op["dma_idx"] + 1
        need_inc = set()
        for op in ops:
            for d in op["deps"]:
                dop = ops[d]
                if dop["dma"]:
                    continue
                if dop["eng"] != op["eng"]:
                    need_inc.add(d)
        ectr = {}
        for op in ops:
            if op["idx"] in need_inc and not op["dma"]:
                ectr[op["eng"]] = ectr.get(op["eng"], 0) + 1
                op["inc"] = ectr[op["eng"]]
        sems = {}
        for e in sorted(set(o["eng"] for o in ops if o["inc"])):
            sems["E_" + e] = ctx.enter_context(nc.semaphore("sem_" + e))
        for q in sorted(qcnt):
            sems["Q_" + q] = ctx.enter_context(nc.semaphore("dma_" + q))
        hw = {}
        for op in ops:
            waits = {}
            for d in op["deps"]:
                dop = ops[d]
                if dop["dma"]:
                    key, val = "Q_" + dop["dma"], 16 * (dop["dma_idx"] + 1)
                elif dop["eng"] == op["eng"]:
                    continue
                else:
                    key, val = "E_" + dop["eng"], dop["inc"]
                    assert val is not None
                if waits.get(key, 0) < val:
                    waits[key] = val
            wl = []
            for key, val in sorted(waits.items()):
                k2 = (op["eng"], key)
                if hw.get(k2, 0) >= val:
                    continue
                hw[k2] = val
                wl.append((key, val))
            op["waits"] = wl
        block = ctx.enter_context(nc.Block())
        for eng in ("sync", "gpsimd", "tensor", "vector", "scalar"):
            eng_ops = [o for o in ops if o["eng"] == eng]
            if not eng_ops:
                continue

            def body(h, eng_ops=eng_ops):
                for o in eng_ops:
                    for key, val in o["waits"]:
                        h.wait_ge(sems[key], val)
                    if o["fn"] is None:
                        continue
                    inst = o["fn"](h)
                    if o["dma"]:
                        inst.then_inc(sems["Q_" + o["dma"]], 16)
                    elif o["inc"]:
                        inst.then_inc(sems["E_" + o["eng"]], 1)

            getattr(block, eng)(body)


def build_nc(stage=STAGE):
    nc = bass.Bass("TRN2", target_bir_lowering=False)
    dt = {}
    def din(name, shape, dtype=F32):
        dt[name] = nc.dram_tensor(name, shape, dtype, kind="ExternalInput").ap()
        return dt[name]

    X = din("x", [NT, H])
    QKVWT = din("qkvwt", [H, 3 * H])
    QKVB24 = din("qkvb24", [24, 128])
    QKVBR = din("qkvbr", [1, H])
    OUTWT = din("outwt", [H, H])
    OUTBR = din("outbr", [1, H])
    RWT = din("rwt", [H, NE])
    EB = din("eb", [128, NE])
    GWT = din("gwt", [NE, H, IM])
    UWT = din("uwt", [NE, H, IM])
    DWT = din("dwt", [NE, IM, H])
    SHGT = din("shgt", [H, H])
    SHUT = din("shut", [H, H])
    SHDT = din("shdt", [H, H])
    IDENT = din("ident", [128, 128])
    CONSTF = din("constf", [128, 18])
    ONES = din("ones", [128, 128])
    OUT = nc.dram_tensor("out", [NT, H], F32, kind="ExternalOutput").ap()
    if DBG:
        DBGS = nc.dram_tensor("dbg_scores", [128, 256], F32, kind="ExternalOutput").ap()
        DBGC = nc.dram_tensor("dbg_comb", [128, 256], F32, kind="ExternalOutput").ap()
        DBGX = nc.dram_tensor("dbg_xlt", [128, NK * NT], BF16, kind="ExternalOutput").ap()
        DBGOT = nc.dram_tensor("dbg_ot", [128, 16384], BF16, kind="ExternalOutput").ap()
        DBGQK = nc.dram_tensor("dbg_qk", [128, 8192], BF16, kind="ExternalOutput").ap()
        DBGVV = nc.dram_tensor("dbg_vv", [128, NTC * 130], BF16, kind="ExternalOutput").ap()

    with ExitStack() as ctx:
        def sb(name, shape, dtp):
            return ctx.enter_context(nc.sbuf_tensor(name, shape, dtp)).ap()
        def psum(name, shape):
            return ctx.enter_context(nc.psum_tensor(name, shape, F32)).ap()

        x_sb = sb("x_sb", [128, NTC * H], F32)
        xlt = sb("xlt", [128, NK * NT], BF16)
        big1 = sb("big1", [128, 16384], BF16)
        r1 = sb("r1", [128, 8192], BF16)
        r2 = sb("r2", [128, 4096], BF16)
        vv = sb("vv", [128, NTC * 130], BF16)
        sc2k = sb("sc2k", [128, 2048], BF16)
        xscr = sc2k.bitcast(F32)
        rcbc = sb("rcbc", [128, 1024], BF16)
        scores = sb("scores", [128, 256], F32)
        comb = sb("comb", [128, 256], F32)
        rt = sb("rt", [128, 144], F32)
        ln_sb = sb("ln_sb", [128, 8], F32)
        bias_cols = sb("bias_cols", [128, 24], F32)
        bqs = sb("bqs", [128, 8], F32)
        qkvb24s = sb("qkvb24s", [24, 128], F32)
        qkvbr_bf = sb("qkvbr_bf", [1, H], BF16)
        outbr_bf = sb("outbr_bf", [1, H], BF16)
        rw_sb = sb("rw_sb", [128, 128], BF16)
        ones_bf = sb("ones_bf", [128, 128], BF16)
        ident_sb = sb("ident_sb", [128, 128], F32)
        constf = sb("constf_sb", [128, 18], F32)
        eb_sb = sb("eb_sb", [128, NE], F32)

        PA = psum("PA", [128, 1024])
        PB = psum("PB", [128, 1024])
        PC = psum("PC", [128, 1024])
        PD = psum("PD", [128, 1024])

        S = Sched()

        iota99 = constf[:, 0:16]
        eps_col = constf[:, 16:17]
        def rtc(a, b):
            return rt[:, a:b]
        GS, TMP4 = rtc(0, 4), rtc(4, 8)
        MCOL, ICOL, JUNK, DEN = rtc(8, 9), rtc(9, 10), rtc(10, 11), rtc(11, 12)
        G1c, G2c, EQ4 = rtc(12, 16), rtc(16, 20), rtc(20, 24)
        SC16, EQ16, A16, OH16, MSK, SM16, CR16 = (
            rtc(32, 48), rtc(48, 64), rtc(64, 80), rtc(80, 96),
            rtc(96, 112), rtc(112, 128), rtc(128, 144))

        def x_ap(tc):
            return x_sb[:, tc * H:(tc + 1) * H]

        def xlt_ap(k, t0, t1):
            return xlt[:, k * NT + t0: k * NT + t1]

        def he_ap(par, ic, t0, t1):
            return big1[:, par * 8192 + ic * NT + t0: par * 8192 + ic * NT + t1]

        def ot_ap(pr, t0, t1, plo=0, phi=128):
            return big1[plo:phi, pr * NT + t0: pr * NT + t1]

        def gw_ap(k, i0, i1):
            return r1[:, k * IM + i0: k * IM + i1]

        def uw_ap(k, i0, i1):
            return r1[:, 4096 + k * IM + i0: 4096 + k * IM + i1]

        def dw_ap(ic, h0, h1):
            return r2[:, ic * H + h0: ic * H + h1]

        def qk_ap(w, t0, t1, plo=0, phi=128):
            return r1[plo:phi, w * NT + t0: w * NT + t1]

        def wo_ap(dc, f0, f1):
            return r1[:, dc * H + f0: dc * H + f1]

        def wqkv_ap(k, w, d0, d1):
            return r2[:, k * 512 + w * 128 + d0: k * 512 + w * 128 + d1]

        def vv_ap(tc, c0, c1):
            return vv[:, tc * 130 + c0: tc * 130 + c1]

        S.add("sync", lambda h: h.dma_start(out=ident_sb, in_=IDENT),
              writes=[("ident",)], dma="sync")
        S.add("sync", lambda h: h.dma_start(out=constf, in_=CONSTF),
              writes=[("constf",)], dma="sync")
        S.add("sync", lambda h: h.dma_start(out=eb_sb, in_=EB),
              writes=[("eb",)], dma="sync")
        S.add("sync", lambda h: h.dma_start(out=qkvb24s, in_=QKVB24),
              writes=[("qkvb24",)], dma="sync")
        S.add("gpsimd", lambda h: h.dma_start(out=ones_bf, in_=ONES),
              writes=[("ones",)], dma="gpsimd")
        S.add("gpsimd", lambda h: h.dma_start(out=rw_sb[:, 0:128],
              in_=RWT.rearrange("(k p) e -> p k e", p=128)),
              writes=[("rw",)], dma="gpsimd")
        S.add("gpsimd", lambda h: h.dma_start(out=qkvbr_bf, in_=QKVBR),
              writes=[("qkvbr",)], dma="gpsimd")
        S.add("gpsimd", lambda h: h.dma_start(out=outbr_bf, in_=OUTBR),
              writes=[("outbr",)], dma="gpsimd")
        Xr = X.rearrange("(c p) h -> p c h", p=128)
        for g in range(4):
            S.add("sync", lambda h, g=g: h.dma_start(
                out=x_sb[:, g * 4 * H:(g + 1) * 4 * H],
                in_=Xr[:, 4 * g:4 * g + 4, :]),
                writes=[("x", 4 * g + j) for j in range(4)], dma="sync")
        S.add("vector", lambda h: h.memset(
            vv.rearrange("p (tc c) -> p tc c", tc=NTC)[:, :, 64:65], 1.0),
            writes=[("vvones",)])
        S.add("vector", lambda h: h.memset(
            vv.rearrange("p (tc c) -> p tc c", tc=NTC)[:, :, 129:130], 1.0),
            writes=[("vvones2",)])

        def emit_ln(nth):
            for tc in range(NTC):
                xt = x_ap(tc)
                s_col, q_col = ln_sb[:, 0:1], ln_sb[:, 1:2]
                m_col, msq = ln_sb[:, 2:3], ln_sb[:, 3:4]
                var, std = ln_sb[:, 4:5], ln_sb[:, 5:6]
                rstd, negmrs = ln_sb[:, 6:7], ln_sb[:, 7:8]
                S.add("vector", lambda h, xt=xt, o=s_col: h.reduce_sum(o, xt, axis=AX.X),
                      reads=[("x", tc)], writes=[("lns",)])
                S.add("scalar", lambda h, xt=xt, o=q_col: h.activation(
                    xscr, xt, AF.Square, accum_out=o),
                    reads=[("x", tc)], writes=[("sc2k", 0), ("sc2k", 1), ("lnq",)])
                S.drain_mode = True
                S.add("vector", lambda h, i=s_col, o=m_col: h.tensor_scalar(
                    o, i, 1.0 / H, None, ALU.mult),
                    reads=[("lns",)], writes=[("lnm",)])
                S.add("vector", lambda h, i=m_col, o=msq: h.tensor_tensor(
                    o, i, i, ALU.mult), reads=[("lnm",)], writes=[("lnmsq",)])
                S.add("vector", lambda h, i=q_col, j=msq, o=var: h.scalar_tensor_tensor(
                    o, i, 1.0 / H, j, ALU.mult, ALU.subtract),
                    reads=[("lnq",), ("lnmsq",)], writes=[("lnvar",)])
                S.add("scalar", lambda h, i=var, o=std: h.activation(
                    o, i, AF.Sqrt, bias=eps_col),
                    reads=[("lnvar",), ("constf",)], writes=[("lnstd",)])
                S.add("vector", lambda h, i=std, o=rstd: h.reciprocal(o, i),
                      reads=[("lnstd",)], writes=[("lnrstd",)])
                S.add("vector", lambda h, i=m_col, j=rstd, o=negmrs:
                      h.scalar_tensor_tensor(o, i, -1.0, j, ALU.mult, ALU.mult),
                      reads=[("lnm",), ("lnrstd",)], writes=[("lnneg",)])
                S.add("scalar", lambda h, xt=xt, sc=rstd, b=negmrs: h.activation(
                    xscr, xt, AF.Identity, bias=b, scale=sc),
                    reads=[("x", tc), ("lnrstd",), ("lnneg",)], writes=[("sc2k", 0), ("sc2k", 1)])
                S.drain_mode = False
                for k in range(NK):
                    S.add("tensor", lambda h, k=k: h.transpose(
                        PD[:, k * 128:(k + 1) * 128],
                        xscr[:, k * 128:(k + 1) * 128], ident_sb),
                        reads=[("sc2k", 0), ("sc2k", 1), ("ident",)], writes=[("PD", k // 4)])
                for hh in range(2):
                    dst = xlt.rearrange("p (k t) -> p k t", k=NK)[
                        :, hh * 4:(hh + 1) * 4, tc * 128:(tc + 1) * 128]
                    S.add("scalar", lambda h, dst=dst, hh=hh: h.activation(
                        dst, PD[:, hh * 512:(hh + 1) * 512], AF.Identity),
                        reads=[("PD", hh)], writes=[("xlt",)])

        def emit_router_mms(call):
            for tc in range(NTC):
                slot = tc % 2
                po = PD[:, slot * 512: slot * 512 + 16]
                for k in range(NK):
                    S.add("tensor", lambda h, k=k, tc=tc, po=po: h.matmul(
                        po, xlt_ap(k, tc * 128, (tc + 1) * 128),
                        rw_sb[:, k * 16:(k + 1) * 16],
                        start=(k == 0), stop=(k == NK - 1)),
                        reads=[("rw",), ("xlt",)], writes=[("PD", slot)])
                S.add("scalar", lambda h, tc=tc, po=po: h.activation(
                    scores[:, tc * 16:(tc + 1) * 16], po, AF.Sigmoid),
                    reads=[("PD", tc % 2)], writes=[("scores", tc)])
        def emit_routing(call):
            PAIRS = [(0, 1), (0, 2), (0, 3), (1, 2), (1, 3), (2, 3)]
            S.drain_mode = True
            for tc in range(NTC):
                sc_tc = scores[:, tc * 16:(tc + 1) * 16]
                V = lambda fn, reads=(), writes=(): S.add("vector", fn, reads, writes)
                V(lambda h, i=sc_tc: h.tensor_tensor(SC16, i, eb_sb, ALU.add),
                  reads=[("scores", tc), ("eb",)], writes=[("rt",)])
                sc16_3 = rt[:, 32:48].rearrange("p (g e) -> p g e", g=4)
                for j, (a, b) in enumerate(PAIRS):
                    if j == 0:
                        V(lambda h, a=a, b=b: h.tensor_tensor(
                            GS, sc16_3[:, :, a], sc16_3[:, :, b], ALU.add))
                    else:
                        V(lambda h, a=a, b=b: h.tensor_tensor(
                            TMP4, sc16_3[:, :, a], sc16_3[:, :, b], ALU.add))
                        V(lambda h: h.tensor_tensor(GS, GS, TMP4, ALU.max))
                for g, Gg in ((1, G1c), (2, G2c)):
                    V(lambda h: h.reduce_max(MCOL, GS, axis=AX.X))
                    V(lambda h: h.tensor_copy(JUNK, MCOL))
                    V(lambda h: h.tensor_single_scalar(EQ4, GS, MCOL, ALU.is_equal))
                    V(lambda h: h.tensor_tensor(TMP4, EQ4, iota99[:, 0:4], ALU.mult),
                      reads=[("constf",)])
                    V(lambda h: h.tensor_reduce(ICOL, TMP4, AX.X, ALU.min))
                    V(lambda h: h.tensor_copy(JUNK, ICOL))
                    V(lambda h, Gg=Gg: h.tensor_single_scalar(
                        Gg, iota99[:, 0:4], ICOL, ALU.is_equal))
                    if g == 1:
                        V(lambda h: h.scalar_tensor_tensor(
                            GS, G1c, -1e9, GS, ALU.mult, ALU.add))
                V(lambda h: h.tensor_tensor(G1c, G1c, G2c, ALU.add))
                sm3 = rt[:, 112:128].rearrange("p (g e) -> p g e", g=4)
                for j in range(4):
                    V(lambda h, j=j: h.tensor_copy(sm3[:, :, j], G1c))
                V(lambda h: h.tensor_tensor(MSK, SC16, SM16, ALU.mult))
                for it in range(4):
                    V(lambda h: h.reduce_max(MCOL, MSK, axis=AX.X))
                    V(lambda h: h.tensor_copy(JUNK, MCOL))
                    V(lambda h: h.tensor_single_scalar(EQ16, MSK, MCOL, ALU.is_equal))
                    V(lambda h: h.tensor_tensor(A16, EQ16, iota99, ALU.mult))
                    V(lambda h: h.tensor_reduce(ICOL, A16, AX.X, ALU.min))
                    V(lambda h: h.tensor_copy(JUNK, ICOL))
                    V(lambda h: h.tensor_single_scalar(OH16, iota99, ICOL, ALU.is_equal))
                    if it == 0:
                        V(lambda h, i=sc_tc: h.tensor_tensor(CR16, OH16, i, ALU.mult),
                          reads=[("scores", tc)])
                    else:
                        V(lambda h, i=sc_tc: h.tensor_tensor(A16, OH16, i, ALU.mult),
                          reads=[("scores", tc)])
                        V(lambda h: h.tensor_tensor(CR16, CR16, A16, ALU.add))
                    V(lambda h: h.scalar_tensor_tensor(
                        MSK, OH16, -1e9, MSK, ALU.mult, ALU.add))
                V(lambda h: h.reduce_sum(DEN, CR16, axis=AX.X))
                V(lambda h: h.tensor_scalar_add(DEN, DEN, 1e-20))
                V(lambda h: h.reciprocal(MCOL, DEN))
                V(lambda h: h.tensor_copy(JUNK, MCOL))
                S.add("vector", lambda h, tc=tc: h.tensor_scalar(
                    comb[:, tc * 16:(tc + 1) * 16], CR16, MCOL, 1.25,
                    ALU.mult, ALU.mult),
                    reads=[("rt",)], writes=[("comb", tc)])
                V(lambda h, tc=tc: h.tensor_copy(
                    JUNK, comb[:, tc * 16:tc * 16 + 1]), reads=[("comb", tc)])
            S.drain_mode = False

        fus_ctr = [0]
        PSL = [(PA, ("PA",)), (PB, ("PB",)), (PC, ("PC",)), (PD, ("PD", 0), ("PD", 1))]

        def pslot(i):
            ent = PSL[i % 4]
            return ent[0], list(ent[1:])

        def emit_round(call, ridx, e):
            par = ridx % 2
            shared = e >= 16
            for ic in range(4):
                for tb2 in range(2):
                    t0 = tb2 * 1024
                    rot = fus_ctr[0] % 2
                    gsl, gtoks = pslot(2 * rot)
                    usl, utoks = pslot(2 * rot + 1)
                    for wtok, wap, ps, ptoks in ((("gw",), gw_ap, gsl, gtoks),
                                                 (("uw",), uw_ap, usl, utoks)):
                        for hh in range(2):
                            for k in range(NK):
                                S.add("tensor", lambda h, k=k, ic=ic, hh=hh, t0=t0,
                                      wap=wap, ps=ps: h.matmul(
                                          ps[:, hh * 512:(hh + 1) * 512],
                                          wap(k, ic * 128, (ic + 1) * 128),
                                          xlt[:, k * NT + t0 + hh * 512:
                                              k * NT + t0 + hh * 512 + 512],
                                          start=(k == 0), stop=(k == NK - 1)),
                                      reads=[wtok, ("xlt",)], writes=ptoks)
                    sp = fus_ctr[0] % 2
                    fus_ctr[0] += 1
                    s1 = sc2k[:, sp * 1024:(sp + 1) * 1024]
                    S.add("scalar", lambda h, s1=s1, gsl=gsl: h.activation(
                        s1, gsl, AF.Silu),
                        reads=gtoks, writes=[("sc2k", sp)])
                    S.add("vector", lambda h, s1=s1, par=par, ic=ic, t0=t0, usl=usl:
                          h.tensor_tensor(he_ap(par, ic, t0, t0 + 1024), s1, usl,
                                          ALU.mult),
                          reads=[("sc2k", sp)] + utoks, writes=[("B1", par)])
            for tc in range(NTC):
                slot, stoks = pslot(tc)
                for hh in range(2):
                    for ic in range(4):
                        S.add("tensor", lambda h, ic=ic, hh=hh, tc=tc, par=par,
                              slot=slot: h.matmul(
                                  slot[:, hh * 512:(hh + 1) * 512],
                                  he_ap(par, ic, tc * 128, (tc + 1) * 128),
                                  dw_ap(ic, hh * 512, hh * 512 + 512),
                                  start=(ic == 0), stop=(ic == 3)),
                              reads=[("B1", par), ("dw",)], writes=stoks)
                if shared:
                    S.add("vector", lambda h, tc=tc, slot=slot:
                          h.scalar_tensor_tensor(x_ap(tc), slot, 0.5, x_ap(tc),
                                                 ALU.mult, ALU.add),
                          reads=stoks + [("x", tc)], writes=[("x", tc)])
                else:
                    ccol = comb[:, tc * 16 + e: tc * 16 + e + 1]
                    S.add("vector", lambda h, tc=tc, slot=slot, ccol=ccol:
                          h.scalar_tensor_tensor(x_ap(tc), slot, ccol, x_ap(tc),
                                                 ALU.mult, ALU.add),
                          reads=stoks + [("x", tc), ("comb", tc)], writes=[("x", tc)])

        def ffn_weight_dmas(call):
            rounds = []
            for ridx, e in enumerate([16, 17] + list(range(NE))):
                if e >= 16:
                    j = e - 16
                    gsrc = SHGT[:, j * IM:(j + 1) * IM]
                    usrc = SHUT[:, j * IM:(j + 1) * IM]
                    dsrc = SHDT[j * IM:(j + 1) * IM, :]
                else:
                    gsrc, usrc, dsrc = GWT[e], UWT[e], DWT[e]
                rounds.append((ridx, e, gsrc, usrc, dsrc))
            for ridx, e, gsrc, usrc, dsrc in rounds:
                S.add("gpsimd", lambda h, s=gsrc: h.dma_start(
                    out=r1[:, 0:4096], in_=s.rearrange("(k p) i -> p k i", p=128)),
                    writes=[("gw",)], dma="gpsimd")
                S.add("gpsimd", lambda h, s=usrc: h.dma_start(
                    out=r1[:, 4096:8192], in_=s.rearrange("(k p) i -> p k i", p=128)),
                    writes=[("uw",)], dma="gpsimd")
                S.add("gpsimd", lambda h, s=dsrc: h.dma_start(
                    out=r2, in_=s.rearrange("(ic p) h -> p ic h", p=128)),
                    writes=[("dw",)], dma="gpsimd")
                yield ridx, e

        def emit_ffn(call):
            emit_router_mms(call)
            gen = ffn_weight_dmas(call)
            for _ in range(2):
                ridx, e = next(gen)
                emit_round(call, ridx, e)
            emit_routing(call)
            for ridx, e in gen:
                emit_round(call, ridx, e)

        def emit_attention():
            S.add("tensor", lambda h: h.transpose(
                PD[:, 0:24], qkvb24s, ident_sb[0:24, 0:24]),
                reads=[("qkvb24",), ("ident",)], writes=[("PD", 0)])
            S.add("vector", lambda h: h.tensor_copy(bias_cols, PD[:, 0:24]),
                  reads=[("PD", 0)], writes=[("bcols",)])
            S.drain_mode = True
            S.add("vector", lambda h: h.tensor_scalar(
                bqs, bias_cols[:, 0:8], 0.125, None, ALU.mult),
                reads=[("bcols",)], writes=[("bqs",)])
            S.drain_mode = False
            def wqkv_dma(pr):
                for w, c0 in ((0, pr * 128), (1, H + pr * 128), (2, 2 * H + pr * 128)):
                    src = QKVWT[:, c0:c0 + 128].rearrange("(k p) d -> p k d", p=128)
                    dst = r2.rearrange("p (k m) -> p k m", k=8)[:, :, w * 128:(w + 1) * 128]
                    S.add("gpsimd", lambda h, src=src, dst=dst: h.dma_start(
                        out=dst, in_=src), writes=[("dw",)], dma="gpsimd")
            wqkv_dma(0)
            for pr in range(8):
                if pr + 1 < 8:
                    pass
                slots = [(PA, ("PA",), 0), (PA, ("PA",), 512), (PB, ("PB",), 0),
                         (PB, ("PB",), 512)]
                for w, name in ((0, "q"), (1, "k")):
                    for tt in range(4):
                        ps, ptok, off = slots[tt]
                        for k in range(NK):
                            S.add("tensor", lambda h, k=k, w=w, tt=tt, ps=ps, off=off:
                                  h.matmul(ps[:, off:off + 512],
                                           wqkv_ap(k, w, 0, 128),
                                           xlt_ap(k, tt * 512, (tt + 1) * 512),
                                           start=(k == 0), stop=(k == NK - 1)),
                                  reads=[("dw",), ("xlt",)], writes=[ptok])
                        if w == 0:
                            bc = bqs[:, pr:pr + 1]
                            scl = 0.125
                            btok = ("bqs",)
                        else:
                            bc = bias_cols[:, 8 + pr:9 + pr]
                            scl = 1.0
                            btok = ("bcols",)
                        S.add("scalar", lambda h, w=w, tt=tt, ps=ps, off=off, bc=bc,
                              scl=scl: h.activation(
                                  qk_ap(w, tt * 512, (tt + 1) * 512), ps[:, off:off + 512],
                                  AF.Identity, bias=bc, scale=scl),
                              reads=[ptok, btok], writes=[("gw",)])
                for tc in range(NTC):
                    po = PD[:, (tc % 2) * 512:(tc % 2) * 512 + 128]
                    ptok = ("PD", tc % 2)
                    for k in range(NK):
                        S.add("tensor", lambda h, k=k, tc=tc, po=po: h.matmul(
                            po, xlt_ap(k, tc * 128, (tc + 1) * 128),
                            wqkv_ap(k, 2, 0, 128),
                            start=(k == 0), stop=False),
                            reads=[("dw",), ("xlt",)], writes=[ptok])
                    S.add("tensor", lambda h, tc=tc, po=po, pr=pr: h.matmul(
                        po, ones_bf[0:1, 0:128],
                        qkvbr_bf[0:1, pr * 128: pr * 128 + 128],
                        start=False, stop=True),
                        reads=[("ones",), ("qkvbr",)], writes=[ptok])
                    dst = vv.rearrange("p (tc b c) -> p tc b c", tc=NTC, b=2)[
                        :, tc, :, 0:64]
                    S.add("scalar", lambda h, dst=dst, po=po: h.activation(
                        dst, po, AF.Identity),
                        reads=[ptok], writes=[("vv", tc)])
                if pr + 1 < 8:
                    wqkv_dma(pr + 1)
                if DBG >= 2 and pr < 7:
                    continue
                if DBG >= 3:
                    continue
                for it in range(4):
                    i0 = it * 512
                    for jg in range(NTC):
                        ps, ptok = (PA, ("PA",)) if jg % 2 == 0 else (PB, ("PB",))
                        S.add("tensor", lambda h, jg=jg, i0=i0, ps=ps: h.matmul(
                            ps[:, 0:512],
                            qk_ap(1, jg * 128, (jg + 1) * 128, 0, 64),
                            qk_ap(0, i0, i0 + 512, 0, 64),
                            start=True, stop=True),
                            reads=[("gw",)], writes=[ptok])
                        S.add("tensor", lambda h, jg=jg, i0=i0, ps=ps: h.matmul(
                            ps[:, 512:1024],
                            qk_ap(1, jg * 128, (jg + 1) * 128, 64, 128),
                            qk_ap(0, i0, i0 + 512, 64, 128),
                            start=True, stop=True),
                            reads=[("gw",)], writes=[ptok])
                        sp = jg % 2
                        pt = sc2k[:, sp * 1024:(sp + 1) * 1024]
                        S.add("scalar", lambda h, pt=pt, ps=ps: h.activation(
                            pt, ps, AF.Exp),
                            reads=[ptok], writes=[("sc2k", sp)])
                        S.add("tensor", lambda h, jg=jg, pt=pt: h.matmul(
                            PC[0:65, 0:512], vv_ap(jg, 0, 65), pt[:, 0:512],
                            start=(jg == 0), stop=(jg == NTC - 1)),
                            reads=[("sc2k", sp), ("vv", jg), ("vvones",)],
                            writes=[("PC",)])
                        S.add("tensor", lambda h, jg=jg, pt=pt: h.matmul(
                            PC[64:128, 512:1024], vv_ap(jg, 65, 129), pt[:, 512:1024],
                            tile_position=(0, 64),
                            start=(jg == 0), stop=(jg == NTC - 1)),
                            reads=[("sc2k", sp), ("vv", jg)], writes=[("PC",)])
                        S.add("tensor", lambda h, jg=jg, pt=pt: h.matmul(
                            PD[0:1, 0:512], ones_bf[:, 0:1], pt[:, 512:1024],
                            start=(jg == 0), stop=(jg == NTC - 1)),
                            reads=[("sc2k", sp), ("ones",)], writes=[("PD", 0)])
                    S.drain_mode = True
                    def _recip(h, o, i):
                        with nc.allow_low_precision(reason="softmax recip feeds bf16 bcast matmul"):
                            return h.reciprocal(o, i)
                    S.add("vector", lambda h: _recip(h,
                        rcbc[64:65, 0:512], PC[64:65, 0:512]),
                        reads=[("PC",)], writes=[("rcbc",)])
                    S.add("vector", lambda h: _recip(h,
                        rcbc[0:1, 512:1024], PD[0:1, 0:512]),
                        reads=[("PD", 0)], writes=[("rcbc",)])
                    S.add("tensor", lambda h: h.matmul(
                        PA[:, 0:512], ones_bf[64:65, 0:128], rcbc[64:65, 0:512],
                        start=True, stop=True),
                        reads=[("rcbc",), ("ones",)], writes=[("PA",)])
                    S.add("tensor", lambda h: h.matmul(
                        PA[:, 512:1024], ones_bf[0:1, 0:128], rcbc[0:1, 512:1024],
                        start=True, stop=True),
                        reads=[("rcbc",), ("ones",)], writes=[("PA",)])
                    S.add("vector", lambda h: h.tensor_copy(rcbc, PA),
                          reads=[("PA",)], writes=[("rcbc",)])
                    S.add("vector", lambda h, pr=pr, i0=i0: h.tensor_tensor(
                        ot_ap(pr, i0, i0 + 512, 0, 64), PC[0:64, 0:512],
                        rcbc[0:64, 0:512], ALU.mult),
                        reads=[("PC",), ("rcbc",)], writes=[("B1", pr // 4)])
                    S.add("vector", lambda h, pr=pr, i0=i0: h.tensor_tensor(
                        ot_ap(pr, i0, i0 + 512, 64, 128), PC[64:128, 512:1024],
                        rcbc[64:128, 512:1024], ALU.mult),
                        reads=[("PC",), ("rcbc",)], writes=[("B1", pr // 4)])
                    S.drain_mode = False
            if DBG >= 2:
                return
            S.add("gpsimd", lambda h: h.dma_start(
                out=r1, in_=OUTWT.rearrange("(dc p) f -> p dc f", p=128)),
                writes=[("gw",), ("uw",)], dma="gpsimd")
            for tc in range(NTC):
                ps, ptok = (PA, ("PA",)) if tc % 2 == 0 else (PB, ("PB",))
                for hh in range(2):
                    for dc in range(NK):
                        S.add("tensor", lambda h, dc=dc, hh=hh, tc=tc, ps=ps: h.matmul(
                            ps[:, hh * 512:(hh + 1) * 512],
                            ot_ap(dc, tc * 128, (tc + 1) * 128),
                            wo_ap(dc, hh * 512, hh * 512 + 512),
                            start=(dc == 0), stop=False),
                            reads=[("B1", dc // 4), ("gw",), ("uw",)], writes=[ptok])
                    S.add("tensor", lambda h, hh=hh, tc=tc, ps=ps: h.matmul(
                        ps[:, hh * 512:(hh + 1) * 512], ones_bf[0:1, 0:128],
                        outbr_bf[0:1, hh * 512: hh * 512 + 512],
                        start=False, stop=True),
                        reads=[("ones",), ("outbr",)], writes=[ptok])
                S.add("vector", lambda h, tc=tc, ps=ps: h.tensor_tensor(
                    x_ap(tc), x_ap(tc), ps, ALU.add),
                    reads=[ptok, ("x", tc)], writes=[("x", tc)])

        if stage >= 1:
            emit_ln(0)
            emit_ffn(0)
        if stage >= 2:
            emit_ln(1)
            emit_attention()
        if stage >= 3:
            emit_ln(2)
            emit_ffn(1)

        if DBG:
            S.add("sync", lambda h: h.dma_start(out=DBGS, in_=scores),
                  reads=[("scores", tc) for tc in range(NTC)], writes=[("OUT",)], dma="sync")
            S.add("sync", lambda h: h.dma_start(out=DBGC, in_=comb),
                  reads=[("comb", tc) for tc in range(NTC)], writes=[("OUT",)], dma="sync")
            S.add("sync", lambda h: h.dma_start(out=DBGX, in_=xlt),
                  reads=[("xlt",)], writes=[("OUT",)], dma="sync")
            if stage >= 2:
                S.add("sync", lambda h: h.dma_start(out=DBGOT, in_=big1),
                      reads=[("B1", 0), ("B1", 1)], writes=[("OUT",)], dma="sync")
                S.add("sync", lambda h: h.dma_start(out=DBGQK, in_=r1),
                      reads=[("gw",), ("uw",)], writes=[("OUT",)], dma="sync")
                S.add("sync", lambda h: h.dma_start(out=DBGVV, in_=vv),
                      reads=[("vv", tc) for tc in range(NTC)], writes=[("OUT",)], dma="sync")
        OUTr = OUT.rearrange("(c p) h -> p c h", p=128)
        for g in range(4):
            S.add("sync", lambda h, g=g: h.dma_start(
                out=OUTr[:, 4 * g:4 * g + 4, :],
                in_=x_sb[:, g * 4 * H:(g + 1) * 4 * H]),
                reads=[("x", 4 * g + j) for j in range(4)],
                writes=[("OUT",)], dma="sync")
        S.add("sync", None, reads=[("OUT",)])

        S.emit(nc, ctx)
    return nc


_NC_CACHE = {}


def _get_nc(stage):
    if stage not in _NC_CACHE:
        _NC_CACHE[stage] = build_nc(stage)
    return _NC_CACHE[stage]


def kernel(L0_attn_x, L0_attn_qkv_w, L0_attn_qkv_b, L0_attn_out_w, L0_attn_out_b,
           L0_ffn_router_weight, L0_ffn_e_bias, L0_ffn_gate_w, L0_ffn_up_w,
           L0_ffn_down_w, L0_ffn_sh_gate_w, L0_ffn_sh_up_w, L0_ffn_sh_down_w):
    f32 = np.float32
    cc = np.ascontiguousarray
    x = cc(np.asarray(L0_attn_x, f32))
    shared = {
        "qkvwt": cc(np.asarray(L0_attn_qkv_w, f32).T),
        "qkvb24": cc(np.asarray(L0_attn_qkv_b, f32).reshape(24, 128)),
        "qkvbr": cc(np.asarray(L0_attn_qkv_b, f32).reshape(1, 3 * H)[:, 2 * H:]),
        "outwt": cc(np.asarray(L0_attn_out_w, f32).T),
        "outbr": cc(np.asarray(L0_attn_out_b, f32).reshape(1, H)),
        "rwt": cc(np.asarray(L0_ffn_router_weight, f32).T),
        "eb": cc(np.tile(np.asarray(L0_ffn_e_bias, f32).reshape(1, NE), (128, 1))),
        "gwt": cc(np.asarray(L0_ffn_gate_w, f32).transpose(0, 2, 1)),
        "uwt": cc(np.asarray(L0_ffn_up_w, f32).transpose(0, 2, 1)),
        "dwt": cc(np.asarray(L0_ffn_down_w, f32).transpose(0, 2, 1)),
        "shgt": cc(np.asarray(L0_ffn_sh_gate_w, f32).T),
        "shut": cc(np.asarray(L0_ffn_sh_up_w, f32).T),
        "shdt": cc(np.asarray(L0_ffn_sh_down_w, f32).T),
        "ident": np.eye(128, dtype=f32),
        "ones": np.ones((128, 128), f32),
    }
    constf = np.zeros((128, 18), f32)
    constf[:, 0:16] = np.arange(16, dtype=f32) - 99.0
    constf[:, 16] = 1e-5
    shared["constf"] = constf
    nc = _get_nc(STAGE)
    in_maps = [dict(shared, x=cc(x[c])) for c in range(8)]
    res = run_bass_kernel_spmd(nc, in_maps, core_ids=list(range(8)))
    return np.stack([r["out"] for r in res.results], axis=0)


# version 04
# speedup vs baseline: 1.0220x; 1.0220x over previous
import math
import numpy as np
import concourse.bass as bass
import concourse.mybir as mybir
from concourse.bass_utils import run_bass_kernel_spmd
from contextlib import ExitStack

F32 = mybir.dt.float32
BF16 = mybir.dt.bfloat16
AF = mybir.ActivationFunctionType
ALU = mybir.AluOpType
AX = mybir.AxisListType

NT = 2048
NTC = 16
H = 1024
NK = 8
NE = 16
IM = 512
NHEAD = 16
HD = 64
STAGE = 3
DBG = 0


class Sched:

    def __init__(self):
        self.ops = []
        self.last_w = {}
        self.readers = {}
        self.drain_mode = False
        self._last_was_drain = {}

    def add(self, eng, fn, reads=(), writes=(), dma=None):
        if (self.drain_mode and fn is not None and dma is None
                and eng in ("vector", "scalar")
                and not self._last_was_drain.get(eng, False)):
            self._last_was_drain[eng] = True
            dr = lambda h: h.drain()
            dr._is_drain = True
            self.add(eng, dr)
        self._last_was_drain[eng] = (fn is not None and dma is None
                                     and getattr(fn, "_is_drain", False))
        idx = len(self.ops)
        deps = set()
        for t in reads:
            if t in self.last_w:
                deps.add(self.last_w[t])
        for t in writes:
            if t in self.last_w:
                deps.add(self.last_w[t])
            for r in self.readers.get(t, ()):
                deps.add(r)
        self.ops.append(dict(idx=idx, eng=eng, fn=fn, deps=deps, dma=dma,
                             inc=None, dma_idx=None, waits=None))
        for t in reads:
            self.readers.setdefault(t, []).append(idx)
        for t in writes:
            self.last_w[t] = idx
            self.readers[t] = []
        return idx

    def emit(self, nc, ctx):
        ops = self.ops
        qcnt = {}
        for op in ops:
            if op["dma"]:
                q = op["dma"]
                op["dma_idx"] = qcnt.get(q, 0)
                qcnt[q] = op["dma_idx"] + 1
        need_inc = set()
        for op in ops:
            for d in op["deps"]:
                dop = ops[d]
                if dop["dma"]:
                    continue
                if dop["eng"] != op["eng"]:
                    need_inc.add(d)
        ectr = {}
        for op in ops:
            if op["idx"] in need_inc and not op["dma"]:
                ectr[op["eng"]] = ectr.get(op["eng"], 0) + 1
                op["inc"] = ectr[op["eng"]]
        sems = {}
        for e in sorted(set(o["eng"] for o in ops if o["inc"])):
            sems["E_" + e] = ctx.enter_context(nc.semaphore("sem_" + e))
        for q in sorted(qcnt):
            sems["Q_" + q] = ctx.enter_context(nc.semaphore("dma_" + q))
        hw = {}
        for op in ops:
            waits = {}
            for d in op["deps"]:
                dop = ops[d]
                if dop["dma"]:
                    key, val = "Q_" + dop["dma"], 16 * (dop["dma_idx"] + 1)
                elif dop["eng"] == op["eng"]:
                    continue
                else:
                    key, val = "E_" + dop["eng"], dop["inc"]
                    assert val is not None
                if waits.get(key, 0) < val:
                    waits[key] = val
            wl = []
            for key, val in sorted(waits.items()):
                k2 = (op["eng"], key)
                if hw.get(k2, 0) >= val:
                    continue
                hw[k2] = val
                wl.append((key, val))
            op["waits"] = wl
        block = ctx.enter_context(nc.Block())
        for eng in ("sync", "gpsimd", "tensor", "vector", "scalar"):
            eng_ops = [o for o in ops if o["eng"] == eng]
            if not eng_ops:
                continue

            def body(h, eng_ops=eng_ops):
                for o in eng_ops:
                    for key, val in o["waits"]:
                        h.wait_ge(sems[key], val)
                    if o["fn"] is None:
                        continue
                    inst = o["fn"](h)
                    if o["dma"]:
                        inst.then_inc(sems["Q_" + o["dma"]], 16)
                    elif o["inc"]:
                        inst.then_inc(sems["E_" + o["eng"]], 1)

            getattr(block, eng)(body)


def build_nc(stage=STAGE):
    nc = bass.Bass("TRN2", target_bir_lowering=False)
    dt = {}
    def din(name, shape, dtype=F32):
        dt[name] = nc.dram_tensor(name, shape, dtype, kind="ExternalInput").ap()
        return dt[name]

    X = din("x", [NT, H])
    QKVWT = din("qkvwt", [H, 3 * H])
    QKVB24 = din("qkvb24", [24, 128])
    QKVBR = din("qkvbr", [1, H])
    OUTWT = din("outwt", [H, H])
    OUTBR = din("outbr", [1, H])
    RWT = din("rwt", [H, NE])
    EB = din("eb", [128, 256])
    IOTAT = din("iotat", [128, 320])
    GWT = din("gwt", [NE, H, IM])
    UWT = din("uwt", [NE, H, IM])
    DWT = din("dwt", [NE, IM, H])
    SHGT = din("shgt", [H, H])
    SHUT = din("shut", [H, H])
    SHDT = din("shdt", [H, H])
    IDENT = din("ident", [128, 128])
    CONSTF = din("constf", [128, 18])
    ONES = din("ones", [128, 128])
    OUT = nc.dram_tensor("out", [NT, H], F32, kind="ExternalOutput").ap()
    if DBG:
        DBGS = nc.dram_tensor("dbg_scores", [128, 256], F32, kind="ExternalOutput").ap()
        DBGC = nc.dram_tensor("dbg_comb", [128, 256], F32, kind="ExternalOutput").ap()
        DBGX = nc.dram_tensor("dbg_xlt", [128, NK * NT], BF16, kind="ExternalOutput").ap()
        DBGOT = nc.dram_tensor("dbg_ot", [128, 16384], BF16, kind="ExternalOutput").ap()
        DBGQK = nc.dram_tensor("dbg_qk", [128, 8192], BF16, kind="ExternalOutput").ap()
        DBGVV = nc.dram_tensor("dbg_vv", [128, NTC * 130], BF16, kind="ExternalOutput").ap()

    with ExitStack() as ctx:
        def sb(name, shape, dtp):
            return ctx.enter_context(nc.sbuf_tensor(name, shape, dtp)).ap()
        def psum(name, shape):
            return ctx.enter_context(nc.psum_tensor(name, shape, F32)).ap()

        x_sb = sb("x_sb", [128, NTC * H], F32)
        xlt = sb("xlt", [128, NK * NT], BF16)
        big1 = sb("big1", [128, 16384], BF16)
        r1 = sb("r1", [128, 8192], BF16)
        r2 = sb("r2", [128, 4096], BF16)
        vv = sb("vv", [128, NTC * 130], BF16)
        sc2k = sb("sc2k", [128, 2048], BF16)
        xscr = sc2k.bitcast(F32)
        rcbc = sb("rcbc", [128, 1024], BF16)
        scores = sb("scores", [128, 256], F32)
        comb = sb("comb", [128, 256], F32)
        rtb = sb("rtb", [128, 1536], F32)
        ln_sb = sb("ln_sb", [128, 8], F32)
        bias_cols = sb("bias_cols", [128, 24], F32)
        bqs = sb("bqs", [128, 8], F32)
        qkvb24s = sb("qkvb24s", [24, 128], F32)
        qkvbr_bf = sb("qkvbr_bf", [1, H], BF16)
        outbr_bf = sb("outbr_bf", [1, H], BF16)
        rw_sb = sb("rw_sb", [128, 128], BF16)
        ones_bf = sb("ones_bf", [128, 128], BF16)
        ident_sb = sb("ident_sb", [128, 128], F32)
        constf = sb("constf_sb", [128, 18], F32)
        eb_sb = sb("eb_sb", [128, 256], F32)
        iotat = sb("iotat_sb", [128, 320], F32)

        PA = psum("PA", [128, 1024])
        PB = psum("PB", [128, 1024])
        PC = psum("PC", [128, 1024])
        PD = psum("PD", [128, 1024])

        S = Sched()

        iota99 = constf[:, 0:16]
        eps_col = constf[:, 16:17]
        iota99 = constf[:, 0:16]
        eps_col = constf[:, 16:17]

        def x_ap(tc):
            return x_sb[:, tc * H:(tc + 1) * H]

        def xlt_ap(k, t0, t1):
            return xlt[:, k * NT + t0: k * NT + t1]

        def he_ap(par, ic, t0, t1):
            return big1[:, par * 8192 + ic * NT + t0: par * 8192 + ic * NT + t1]

        def ot_ap(pr, t0, t1, plo=0, phi=128):
            return big1[plo:phi, pr * NT + t0: pr * NT + t1]

        def gw_ap(k, i0, i1):
            return r1[:, k * IM + i0: k * IM + i1]

        def uw_ap(k, i0, i1):
            return r1[:, 4096 + k * IM + i0: 4096 + k * IM + i1]

        def dw_ap(ic, h0, h1):
            return r2[:, ic * H + h0: ic * H + h1]

        def qk_ap(w, t0, t1, plo=0, phi=128):
            return r1[plo:phi, w * NT + t0: w * NT + t1]

        def wo_ap(dc, f0, f1):
            return r1[:, dc * H + f0: dc * H + f1]

        def wqkv_ap(k, w, d0, d1):
            return r2[:, k * 512 + w * 128 + d0: k * 512 + w * 128 + d1]

        def vv_ap(tc, c0, c1):
            return vv[:, tc * 130 + c0: tc * 130 + c1]

        S.add("sync", lambda h: h.dma_start(out=ident_sb, in_=IDENT),
              writes=[("ident",)], dma="sync")
        S.add("sync", lambda h: h.dma_start(out=constf, in_=CONSTF),
              writes=[("constf",)], dma="sync")
        S.add("sync", lambda h: h.dma_start(out=eb_sb, in_=EB),
              writes=[("eb",)], dma="sync")
        S.add("sync", lambda h: h.dma_start(out=iotat, in_=IOTAT),
              writes=[("iotat",)], dma="sync")
        S.add("sync", lambda h: h.dma_start(out=qkvb24s, in_=QKVB24),
              writes=[("qkvb24",)], dma="sync")
        S.add("gpsimd", lambda h: h.dma_start(out=ones_bf, in_=ONES),
              writes=[("ones",)], dma="gpsimd")
        S.add("gpsimd", lambda h: h.dma_start(out=rw_sb[:, 0:128],
              in_=RWT.rearrange("(k p) e -> p k e", p=128)),
              writes=[("rw",)], dma="gpsimd")
        S.add("gpsimd", lambda h: h.dma_start(out=qkvbr_bf, in_=QKVBR),
              writes=[("qkvbr",)], dma="gpsimd")
        S.add("gpsimd", lambda h: h.dma_start(out=outbr_bf, in_=OUTBR),
              writes=[("outbr",)], dma="gpsimd")
        Xr = X.rearrange("(c p) h -> p c h", p=128)
        for g in range(4):
            S.add("sync", lambda h, g=g: h.dma_start(
                out=x_sb[:, g * 4 * H:(g + 1) * 4 * H],
                in_=Xr[:, 4 * g:4 * g + 4, :]),
                writes=[("x", 4 * g + j) for j in range(4)], dma="sync")

        def emit_ln(nth):
            for tc in range(NTC):
                xt = x_ap(tc)
                s_col, q_col = ln_sb[:, 0:1], ln_sb[:, 1:2]
                m_col, msq = ln_sb[:, 2:3], ln_sb[:, 3:4]
                var, std = ln_sb[:, 4:5], ln_sb[:, 5:6]
                rstd, negmrs = ln_sb[:, 6:7], ln_sb[:, 7:8]
                S.add("vector", lambda h, xt=xt, o=s_col: h.reduce_sum(o, xt, axis=AX.X),
                      reads=[("x", tc)], writes=[("lns",)])
                S.add("scalar", lambda h, xt=xt, o=q_col: h.activation(
                    xscr, xt, AF.Square, accum_out=o),
                    reads=[("x", tc)], writes=[("sc2k", 0), ("sc2k", 1), ("lnq",)])
                S.drain_mode = True
                S.add("vector", lambda h, i=s_col, o=m_col: h.tensor_scalar(
                    o, i, 1.0 / H, None, ALU.mult),
                    reads=[("lns",)], writes=[("lnm",)])
                S.add("vector", lambda h, i=m_col, o=msq: h.tensor_tensor(
                    o, i, i, ALU.mult), reads=[("lnm",)], writes=[("lnmsq",)])
                S.add("vector", lambda h, i=q_col, j=msq, o=var: h.scalar_tensor_tensor(
                    o, i, 1.0 / H, j, ALU.mult, ALU.subtract),
                    reads=[("lnq",), ("lnmsq",)], writes=[("lnvar",)])
                S.add("scalar", lambda h, i=var, o=std: h.activation(
                    o, i, AF.Sqrt, bias=eps_col),
                    reads=[("lnvar",), ("constf",)], writes=[("lnstd",)])
                S.add("vector", lambda h, i=std, o=rstd: h.reciprocal(o, i),
                      reads=[("lnstd",)], writes=[("lnrstd",)])
                S.add("vector", lambda h, i=m_col, j=rstd, o=negmrs:
                      h.scalar_tensor_tensor(o, i, -1.0, j, ALU.mult, ALU.mult),
                      reads=[("lnm",), ("lnrstd",)], writes=[("lnneg",)])
                S.add("scalar", lambda h, xt=xt, sc=rstd, b=negmrs: h.activation(
                    xscr, xt, AF.Identity, bias=b, scale=sc),
                    reads=[("x", tc), ("lnrstd",), ("lnneg",)], writes=[("sc2k", 0), ("sc2k", 1)])
                S.drain_mode = False
                for k in range(NK):
                    S.add("tensor", lambda h, k=k: h.transpose(
                        PD[:, k * 128:(k + 1) * 128],
                        xscr[:, k * 128:(k + 1) * 128], ident_sb),
                        reads=[("sc2k", 0), ("sc2k", 1), ("ident",)], writes=[("PD", k // 4)])
                for hh in range(2):
                    dst = xlt.rearrange("p (k t) -> p k t", k=NK)[
                        :, hh * 4:(hh + 1) * 4, tc * 128:(tc + 1) * 128]
                    S.add("scalar", lambda h, dst=dst, hh=hh: h.activation(
                        dst, PD[:, hh * 512:(hh + 1) * 512], AF.Identity),
                        reads=[("PD", hh)], writes=[("xlt",)])

        def emit_router_mms(call):
            for tc in range(NTC):
                slot = tc % 2
                po = PD[:, slot * 512: slot * 512 + 16]
                for k in range(NK):
                    S.add("tensor", lambda h, k=k, tc=tc, po=po: h.matmul(
                        po, xlt_ap(k, tc * 128, (tc + 1) * 128),
                        rw_sb[:, k * 16:(k + 1) * 16],
                        start=(k == 0), stop=(k == NK - 1)),
                        reads=[("rw",), ("xlt",)], writes=[("PD", slot)])
                S.add("scalar", lambda h, tc=tc, po=po: h.activation(
                    scores[:, tc * 16:(tc + 1) * 16], po, AF.Sigmoid),
                    reads=[("PD", tc % 2)], writes=[("scores", tc)])
        def emit_routing(call):
            rtb = vv.bitcast(F32)
            rtc = rcbc.bitcast(F32)
            SC256 = rtb[:, 0:256]
            MSK256 = rtb[:, 256:512]
            OHS256 = rtb[:, 512:768]
            EQ256 = rtb[:, 768:1024]
            GS64 = rtc[:, 0:64]
            T64a = rtc[:, 64:128]
            T64b = rtc[:, 128:192]
            M16 = rtc[:, 192:208]
            I16 = rtc[:, 208:224]
            D16 = rtc[:, 224:240]
            JK = rtc[:, 240:241]
            iota4t = iotat[:, 0:64]
            iota16t = iotat[:, 64:320]
            S.drain_mode = True
            V = lambda fn, reads=(), writes=(): S.add(
                "vector", fn, list(reads) + [("rtb",)], list(writes) + [("rtb",)])
            sc_reads = [("scores", tc) for tc in range(NTC)]
            V(lambda h: h.tensor_tensor(SC256, scores, eb_sb, ALU.add),
              reads=sc_reads + [("eb",)],
              writes=[("vv", t) for t in range(NTC)] + [("rcbc",)])
            sc4 = SC256.rearrange("p (c g e) -> p c g e", c=16, g=4)
            g3 = GS64.rearrange("p (c g) -> p c g", c=16)
            t3a = T64a.rearrange("p (c g) -> p c g", c=16)
            PAIRS = [(0, 1), (0, 2), (0, 3), (1, 2), (1, 3), (2, 3)]
            for j, (a, b) in enumerate(PAIRS):
                if j == 0:
                    V(lambda h, a=a, b=b: h.tensor_tensor(
                        g3, sc4[:, :, :, a], sc4[:, :, :, b], ALU.add))
                else:
                    V(lambda h, a=a, b=b: h.tensor_tensor(
                        t3a, sc4[:, :, :, a], sc4[:, :, :, b], ALU.add))
                    V(lambda h: h.tensor_tensor(GS64, GS64, T64a, ALU.max))
            i4t3 = iota4t.rearrange("p (c g) -> p c g", c=16)
            m16b4 = M16.unsqueeze(2).broadcast_to([128, 16, 4])
            i16b4 = I16.unsqueeze(2).broadcast_to([128, 16, 4])
            for grp, dst in ((1, T64a), (2, T64b)):
                d3 = dst.rearrange("p (c g) -> p c g", c=16)
                V(lambda h: h.tensor_reduce(M16, g3, AX.X, ALU.max))
                V(lambda h: h.tensor_copy(JK, M16[:, 0:1]))
                V(lambda h, d3=d3: h.tensor_tensor(d3, g3, m16b4, ALU.is_equal))
                V(lambda h, dst=dst: h.tensor_tensor(dst, dst, iota4t, ALU.mult))
                V(lambda h, d3=d3: h.tensor_reduce(I16, d3, AX.X, ALU.min))
                V(lambda h: h.tensor_copy(JK, I16[:, 0:1]))
                V(lambda h, d3=d3: h.tensor_tensor(d3, i4t3, i16b4, ALU.is_equal))
                if grp == 1:
                    V(lambda h: h.scalar_tensor_tensor(
                        GS64, T64a, -1e9, GS64, ALU.mult, ALU.add))
            V(lambda h: h.tensor_tensor(T64a, T64a, T64b, ALU.add))
            sm4 = MSK256.rearrange("p (c g e) -> p c g e", c=16, g=4)
            for j in range(4):
                V(lambda h, j=j: h.tensor_copy(sm4[:, :, :, j], t3a))
            V(lambda h: h.tensor_tensor(MSK256, SC256, MSK256, ALU.mult))
            m3 = MSK256.rearrange("p (c e) -> p c e", c=16)
            e3 = EQ256.rearrange("p (c e) -> p c e", c=16)
            a3 = SC256.rearrange("p (c e) -> p c e", c=16)
            i16t3 = iota16t.rearrange("p (c e) -> p c e", c=16)
            m16b16 = M16.unsqueeze(2).broadcast_to([128, 16, 16])
            i16b16 = I16.unsqueeze(2).broadcast_to([128, 16, 16])
            for it in range(4):
                V(lambda h: h.tensor_reduce(M16, m3, AX.X, ALU.max))
                V(lambda h: h.tensor_copy(JK, M16[:, 0:1]))
                V(lambda h: h.tensor_tensor(e3, m3, m16b16, ALU.is_equal))
                V(lambda h: h.tensor_tensor(SC256, EQ256, iota16t, ALU.mult))
                V(lambda h: h.tensor_reduce(I16, a3, AX.X, ALU.min))
                V(lambda h: h.tensor_copy(JK, I16[:, 0:1]))
                V(lambda h: h.tensor_tensor(e3, i16t3, i16b16, ALU.is_equal))
                if it == 0:
                    V(lambda h: h.tensor_copy(OHS256, EQ256))
                else:
                    V(lambda h: h.tensor_tensor(OHS256, OHS256, EQ256, ALU.add))
                V(lambda h: h.scalar_tensor_tensor(
                    MSK256, EQ256, -1e9, MSK256, ALU.mult, ALU.add))
            V(lambda h: h.tensor_tensor(EQ256, OHS256, scores, ALU.mult),
              reads=sc_reads)
            V(lambda h: h.tensor_reduce(D16, e3, AX.X, ALU.add))
            V(lambda h: h.tensor_scalar_add(D16, D16, 1e-20))
            V(lambda h: h.reciprocal(M16, D16))
            V(lambda h: h.tensor_copy(JK, M16[:, 0:1]))
            V(lambda h: h.scalar_tensor_tensor(
                comb.rearrange("p (c e) -> p c e", c=16), e3, 1.25, m16b16,
                ALU.mult, ALU.mult), writes=[("comb",)])
            V(lambda h: h.tensor_copy(JK, comb[:, 0:1]), reads=[("comb",)])
            S.drain_mode = False

        fus_ctr = [0]

        def gu_steps(ridx, e):
            par = ridx % 2
            steps = []
            for ic in range(4):
                for tb2 in range(2):
                    def step(ic=ic, tb2=tb2, par=par):
                        t0 = tb2 * 1024
                        for wname, wap, ps, ptok in (
                                ("gw", gw_ap, PA, ("PA",)),
                                ("uw", uw_ap, PB, ("PB",))):
                            wtok = (wname, ic // 2)
                            for hh in range(2):
                                for k in range(NK):
                                    S.add("tensor", lambda h, k=k, ic=ic, hh=hh,
                                          t0=t0, wap=wap, ps=ps: h.matmul(
                                              ps[:, hh * 512:(hh + 1) * 512],
                                              wap(k, ic * 128, (ic + 1) * 128),
                                              xlt[:, k * NT + t0 + hh * 512:
                                                  k * NT + t0 + hh * 512 + 512],
                                              start=(k == 0), stop=(k == NK - 1)),
                                          reads=[wtok, ("xlt",)], writes=[ptok])
                        sp = fus_ctr[0] % 2
                        fus_ctr[0] += 1
                        s1 = sc2k[:, sp * 1024:(sp + 1) * 1024]
                        S.add("scalar", lambda h, s1=s1: h.activation(s1, PA, AF.Silu),
                              reads=[("PA",)], writes=[("sc2k", sp)])
                        S.add("vector", lambda h, s1=s1, par=par, ic=ic, t0=t0:
                              h.tensor_tensor(he_ap(par, ic, t0, t0 + 1024), s1, PB,
                                              ALU.mult),
                              reads=[("sc2k", sp), ("PB",)], writes=[("B1", par)])
                    steps.append(step)
            return steps

        def down_steps(ridx, e):
            par = ridx % 2
            shared = e >= 16
            steps = []
            for tc in range(NTC):
                def step(tc=tc, par=par, shared=shared, e=e):
                    if tc % 2 == 0:
                        slot, stoks = PC, [("PC",)]
                    else:
                        slot, stoks = PD, [("PD", 0), ("PD", 1)]
                    for hh in range(2):
                        for ic in range(4):
                            S.add("tensor", lambda h, ic=ic, hh=hh, tc=tc, par=par,
                                  slot=slot: h.matmul(
                                      slot[:, hh * 512:(hh + 1) * 512],
                                      he_ap(par, ic, tc * 128, (tc + 1) * 128),
                                      dw_ap(ic, hh * 512, hh * 512 + 512),
                                      start=(ic == 0), stop=(ic == 3)),
                                  reads=[("B1", par), ("dw",)], writes=stoks)
                    if shared:
                        S.add("vector", lambda h, tc=tc, slot=slot:
                              h.scalar_tensor_tensor(x_ap(tc), slot, 0.5, x_ap(tc),
                                                     ALU.mult, ALU.add),
                              reads=stoks + [("x", tc)], writes=[("x", tc)])
                    else:
                        ccol = comb[:, tc * 16 + e: tc * 16 + e + 1]
                        S.add("vector", lambda h, tc=tc, slot=slot, ccol=ccol:
                              h.scalar_tensor_tensor(x_ap(tc), slot, ccol, x_ap(tc),
                                                     ALU.mult, ALU.add),
                              reads=stoks + [("x", tc), ("comb",)],
                              writes=[("x", tc)])
                steps.append(step)
            return steps

        def emit_ffn(call):
            emit_router_mms(call)
            rounds = [16, 17] + list(range(NE))
            srcs = []
            for e in rounds:
                if e >= 16:
                    j = e - 16
                    srcs.append((SHGT[:, j * IM:(j + 1) * IM],
                                 SHUT[:, j * IM:(j + 1) * IM],
                                 SHDT[j * IM:(j + 1) * IM, :]))
                else:
                    srcs.append((GWT[e], UWT[e], DWT[e]))
            pend = []
            for i, e in enumerate(rounds):
                gsrc, usrc, dsrc = srcs[i]
                r1v = r1.rearrange("p (w k i) -> p w k i", w=2, k=8)
                for hf in range(2):
                    S.add("gpsimd", lambda h, s=gsrc, hf=hf: h.dma_start(
                        out=r1v[:, 0, :, hf * 256:(hf + 1) * 256],
                        in_=s[:, hf * 256:(hf + 1) * 256].rearrange(
                            "(k p) i -> p k i", p=128)),
                        writes=[("gw", hf)], dma="gpsimd")
                    S.add("gpsimd", lambda h, s=usrc, hf=hf: h.dma_start(
                        out=r1v[:, 1, :, hf * 256:(hf + 1) * 256],
                        in_=s[:, hf * 256:(hf + 1) * 256].rearrange(
                            "(k p) i -> p k i", p=128)),
                        writes=[("uw", hf)], dma="gpsimd")
                gus = gu_steps(i, e)
                for j, g in enumerate(gus):
                    g()
                    for d in pend[2 * j:2 * j + 2]:
                        d()
                if i == 0:
                    emit_routing(call)
                S.add("gpsimd", lambda h, s=dsrc: h.dma_start(
                    out=r2, in_=s.rearrange("(ic p) h -> p ic h", p=128)),
                    writes=[("dw",)], dma="gpsimd")
                pend = down_steps(i, e)
            for d in pend:
                d()

        def emit_attention():
            S.add("vector", lambda h: h.memset(
                vv.rearrange("p (tc c) -> p tc c", tc=NTC)[:, :, 64:65], 1.0),
                writes=[("vvones",), ("rtb",)])
            S.add("vector", lambda h: h.memset(
                vv.rearrange("p (tc c) -> p tc c", tc=NTC)[:, :, 129:130], 1.0),
                writes=[("vvones2",), ("rtb",)])
            S.add("tensor", lambda h: h.transpose(
                PD[:, 0:24], qkvb24s, ident_sb[0:24, 0:24]),
                reads=[("qkvb24",), ("ident",)], writes=[("PD", 0)])
            S.add("vector", lambda h: h.tensor_copy(bias_cols, PD[:, 0:24]),
                  reads=[("PD", 0)], writes=[("bcols",)])
            S.drain_mode = True
            S.add("vector", lambda h: h.tensor_scalar(
                bqs, bias_cols[:, 0:8], 0.125, None, ALU.mult),
                reads=[("bcols",)], writes=[("bqs",)])
            S.drain_mode = False
            def wqkv_dma(pr):
                for w, c0 in ((0, pr * 128), (1, H + pr * 128), (2, 2 * H + pr * 128)):
                    src = QKVWT[:, c0:c0 + 128].rearrange("(k p) d -> p k d", p=128)
                    dst = r2.rearrange("p (k m) -> p k m", k=8)[:, :, w * 128:(w + 1) * 128]
                    S.add("gpsimd", lambda h, src=src, dst=dst: h.dma_start(
                        out=dst, in_=src), writes=[("dw",)], dma="gpsimd")
            wqkv_dma(0)
            for pr in range(8):
                if pr + 1 < 8:
                    pass
                slots = [(PA, ("PA",), 0), (PA, ("PA",), 512), (PB, ("PB",), 0),
                         (PB, ("PB",), 512)]
                for w, name in ((0, "q"), (1, "k")):
                    for tt in range(4):
                        ps, ptok, off = slots[tt]
                        for k in range(NK):
                            S.add("tensor", lambda h, k=k, w=w, tt=tt, ps=ps, off=off:
                                  h.matmul(ps[:, off:off + 512],
                                           wqkv_ap(k, w, 0, 128),
                                           xlt_ap(k, tt * 512, (tt + 1) * 512),
                                           start=(k == 0), stop=(k == NK - 1)),
                                  reads=[("dw",), ("xlt",)], writes=[ptok])
                        if w == 0:
                            bc = bqs[:, pr:pr + 1]
                            scl = 0.125
                            btok = ("bqs",)
                        else:
                            bc = bias_cols[:, 8 + pr:9 + pr]
                            scl = 1.0
                            btok = ("bcols",)
                        S.add("scalar", lambda h, w=w, tt=tt, ps=ps, off=off, bc=bc,
                              scl=scl: h.activation(
                                  qk_ap(w, tt * 512, (tt + 1) * 512), ps[:, off:off + 512],
                                  AF.Identity, bias=bc, scale=scl),
                              reads=[ptok, btok], writes=[("gw", 0), ("gw", 1)])
                for tc in range(NTC):
                    po = PD[:, (tc % 2) * 512:(tc % 2) * 512 + 128]
                    ptok = ("PD", tc % 2)
                    for k in range(NK):
                        S.add("tensor", lambda h, k=k, tc=tc, po=po: h.matmul(
                            po, xlt_ap(k, tc * 128, (tc + 1) * 128),
                            wqkv_ap(k, 2, 0, 128),
                            start=(k == 0), stop=False),
                            reads=[("dw",), ("xlt",)], writes=[ptok])
                    S.add("tensor", lambda h, tc=tc, po=po, pr=pr: h.matmul(
                        po, ones_bf[0:1, 0:128],
                        qkvbr_bf[0:1, pr * 128: pr * 128 + 128],
                        start=False, stop=True),
                        reads=[("ones",), ("qkvbr",)], writes=[ptok])
                    dst = vv.rearrange("p (tc b c) -> p tc b c", tc=NTC, b=2)[
                        :, tc, :, 0:64]
                    S.add("scalar", lambda h, dst=dst, po=po: h.activation(
                        dst, po, AF.Identity),
                        reads=[ptok], writes=[("vv", tc), ("rtb",)])
                if pr + 1 < 8:
                    wqkv_dma(pr + 1)
                if DBG >= 2 and pr < 7:
                    continue
                if DBG >= 3:
                    continue
                for it in range(4):
                    i0 = it * 512
                    for jg in range(NTC):
                        ps, ptok = (PA, ("PA",)) if jg % 2 == 0 else (PB, ("PB",))
                        S.add("tensor", lambda h, jg=jg, i0=i0, ps=ps: h.matmul(
                            ps[:, 0:512],
                            qk_ap(1, jg * 128, (jg + 1) * 128, 0, 64),
                            qk_ap(0, i0, i0 + 512, 0, 64),
                            start=True, stop=True),
                            reads=[("gw", 0), ("gw", 1)], writes=[ptok])
                        S.add("tensor", lambda h, jg=jg, i0=i0, ps=ps: h.matmul(
                            ps[:, 512:1024],
                            qk_ap(1, jg * 128, (jg + 1) * 128, 64, 128),
                            qk_ap(0, i0, i0 + 512, 64, 128),
                            start=True, stop=True),
                            reads=[("gw", 0), ("gw", 1)], writes=[ptok])
                        sp = jg % 2
                        pt = sc2k[:, sp * 1024:(sp + 1) * 1024]
                        S.add("scalar", lambda h, pt=pt, ps=ps: h.activation(
                            pt, ps, AF.Exp),
                            reads=[ptok], writes=[("sc2k", sp)])
                        S.add("tensor", lambda h, jg=jg, pt=pt: h.matmul(
                            PC[0:65, 0:512], vv_ap(jg, 0, 65), pt[:, 0:512],
                            start=(jg == 0), stop=(jg == NTC - 1)),
                            reads=[("sc2k", sp), ("vv", jg), ("vvones",)],
                            writes=[("PC",)])
                        S.add("tensor", lambda h, jg=jg, pt=pt: h.matmul(
                            PC[64:128, 512:1024], vv_ap(jg, 65, 129), pt[:, 512:1024],
                            tile_position=(0, 64),
                            start=(jg == 0), stop=(jg == NTC - 1)),
                            reads=[("sc2k", sp), ("vv", jg)], writes=[("PC",)])
                        S.add("tensor", lambda h, jg=jg, pt=pt: h.matmul(
                            PD[0:1, 0:512], ones_bf[:, 0:1], pt[:, 512:1024],
                            start=(jg == 0), stop=(jg == NTC - 1)),
                            reads=[("sc2k", sp), ("ones",)], writes=[("PD", 0)])
                    S.drain_mode = True
                    def _recip(h, o, i):
                        with nc.allow_low_precision(reason="softmax recip feeds bf16 bcast matmul"):
                            return h.reciprocal(o, i)
                    S.add("vector", lambda h: _recip(h,
                        rcbc[64:65, 0:512], PC[64:65, 0:512]),
                        reads=[("PC",)], writes=[("rcbc",), ("rtb",)])
                    S.add("vector", lambda h: _recip(h,
                        rcbc[0:1, 512:1024], PD[0:1, 0:512]),
                        reads=[("PD", 0)], writes=[("rcbc",), ("rtb",)])
                    S.add("tensor", lambda h: h.matmul(
                        PA[:, 0:512], ones_bf[64:65, 0:128], rcbc[64:65, 0:512],
                        start=True, stop=True),
                        reads=[("rcbc",), ("ones",)], writes=[("PA",)])
                    S.add("tensor", lambda h: h.matmul(
                        PA[:, 512:1024], ones_bf[0:1, 0:128], rcbc[0:1, 512:1024],
                        start=True, stop=True),
                        reads=[("rcbc",), ("ones",)], writes=[("PA",)])
                    S.add("vector", lambda h: h.tensor_copy(rcbc, PA),
                          reads=[("PA",)], writes=[("rcbc",), ("rtb",)])
                    S.add("vector", lambda h, pr=pr, i0=i0: h.tensor_tensor(
                        ot_ap(pr, i0, i0 + 512, 0, 64), PC[0:64, 0:512],
                        rcbc[0:64, 0:512], ALU.mult),
                        reads=[("PC",), ("rcbc",)], writes=[("B1", pr // 4)])
                    S.add("vector", lambda h, pr=pr, i0=i0: h.tensor_tensor(
                        ot_ap(pr, i0, i0 + 512, 64, 128), PC[64:128, 512:1024],
                        rcbc[64:128, 512:1024], ALU.mult),
                        reads=[("PC",), ("rcbc",)], writes=[("B1", pr // 4)])
                    S.drain_mode = False
            if DBG >= 2:
                return
            S.add("gpsimd", lambda h: h.dma_start(
                out=r1, in_=OUTWT.rearrange("(dc p) f -> p dc f", p=128)),
                writes=[("gw", 0), ("gw", 1), ("uw", 0), ("uw", 1)], dma="gpsimd")
            for tc in range(NTC):
                ps, ptok = (PA, ("PA",)) if tc % 2 == 0 else (PB, ("PB",))
                for hh in range(2):
                    for dc in range(NK):
                        S.add("tensor", lambda h, dc=dc, hh=hh, tc=tc, ps=ps: h.matmul(
                            ps[:, hh * 512:(hh + 1) * 512],
                            ot_ap(dc, tc * 128, (tc + 1) * 128),
                            wo_ap(dc, hh * 512, hh * 512 + 512),
                            start=(dc == 0), stop=False),
                            reads=[("B1", dc // 4), ("gw", 0), ("gw", 1), ("uw", 0), ("uw", 1)], writes=[ptok])
                    S.add("tensor", lambda h, hh=hh, tc=tc, ps=ps: h.matmul(
                        ps[:, hh * 512:(hh + 1) * 512], ones_bf[0:1, 0:128],
                        outbr_bf[0:1, hh * 512: hh * 512 + 512],
                        start=False, stop=True),
                        reads=[("ones",), ("outbr",)], writes=[ptok])
                S.add("vector", lambda h, tc=tc, ps=ps: h.tensor_tensor(
                    x_ap(tc), x_ap(tc), ps, ALU.add),
                    reads=[ptok, ("x", tc)], writes=[("x", tc)])

        if stage >= 1:
            emit_ln(0)
            emit_ffn(0)
        if stage >= 2:
            emit_ln(1)
            emit_attention()
        if stage >= 3:
            emit_ln(2)
            emit_ffn(1)

        if DBG:
            S.add("sync", lambda h: h.dma_start(out=DBGS, in_=scores),
                  reads=[("scores", tc) for tc in range(NTC)], writes=[("OUT",)], dma="sync")
            S.add("sync", lambda h: h.dma_start(out=DBGC, in_=comb),
                  reads=[("comb", tc) for tc in range(NTC)], writes=[("OUT",)], dma="sync")
            S.add("sync", lambda h: h.dma_start(out=DBGX, in_=xlt),
                  reads=[("xlt",)], writes=[("OUT",)], dma="sync")
            if stage >= 2:
                S.add("sync", lambda h: h.dma_start(out=DBGOT, in_=big1),
                      reads=[("B1", 0), ("B1", 1)], writes=[("OUT",)], dma="sync")
                S.add("sync", lambda h: h.dma_start(out=DBGQK, in_=r1),
                      reads=[("gw", 0), ("gw", 1), ("uw", 0), ("uw", 1)], writes=[("OUT",)], dma="sync")
                S.add("sync", lambda h: h.dma_start(out=DBGVV, in_=vv),
                      reads=[("vv", tc) for tc in range(NTC)], writes=[("OUT",)], dma="sync")
        OUTr = OUT.rearrange("(c p) h -> p c h", p=128)
        for g in range(4):
            S.add("sync", lambda h, g=g: h.dma_start(
                out=OUTr[:, 4 * g:4 * g + 4, :],
                in_=x_sb[:, g * 4 * H:(g + 1) * 4 * H]),
                reads=[("x", 4 * g + j) for j in range(4)],
                writes=[("OUT",)], dma="sync")
        S.add("sync", None, reads=[("OUT",)])

        S.emit(nc, ctx)
    return nc


_NC_CACHE = {}


def _get_nc(stage):
    if stage not in _NC_CACHE:
        _NC_CACHE[stage] = build_nc(stage)
    return _NC_CACHE[stage]


def host_shared(L0_attn_qkv_w, L0_attn_qkv_b, L0_attn_out_w, L0_attn_out_b,
                L0_ffn_router_weight, L0_ffn_e_bias, L0_ffn_gate_w, L0_ffn_up_w,
                L0_ffn_down_w, L0_ffn_sh_gate_w, L0_ffn_sh_up_w, L0_ffn_sh_down_w,
                **_ignored):
    f32 = np.float32
    cc = np.ascontiguousarray
    shared = {
        "qkvwt": cc(np.asarray(L0_attn_qkv_w, f32).T),
        "qkvb24": cc(np.asarray(L0_attn_qkv_b, f32).reshape(24, 128)),
        "qkvbr": cc(np.asarray(L0_attn_qkv_b, f32).reshape(1, 3 * H)[:, 2 * H:]),
        "outwt": cc(np.asarray(L0_attn_out_w, f32).T),
        "outbr": cc(np.asarray(L0_attn_out_b, f32).reshape(1, H)),
        "rwt": cc(np.asarray(L0_ffn_router_weight, f32).T),
        "eb": cc(np.tile(np.asarray(L0_ffn_e_bias, f32).reshape(1, NE), (128, 16))),
        "gwt": cc(np.asarray(L0_ffn_gate_w, f32).transpose(0, 2, 1)),
        "uwt": cc(np.asarray(L0_ffn_up_w, f32).transpose(0, 2, 1)),
        "dwt": cc(np.asarray(L0_ffn_down_w, f32).transpose(0, 2, 1)),
        "shgt": cc(np.asarray(L0_ffn_sh_gate_w, f32).T),
        "shut": cc(np.asarray(L0_ffn_sh_up_w, f32).T),
        "shdt": cc(np.asarray(L0_ffn_sh_down_w, f32).T),
        "ident": np.eye(128, dtype=f32),
        "ones": np.ones((128, 128), f32),
    }
    constf = np.zeros((128, 18), f32)
    constf[:, 0:16] = np.arange(16, dtype=f32) - 99.0
    constf[:, 16] = 1e-5
    shared["constf"] = constf
    iotat = np.zeros((128, 320), f32)
    iotat[:, 0:64] = np.tile(np.arange(4, dtype=f32) - 99.0, (128, 16))
    iotat[:, 64:320] = np.tile(np.arange(16, dtype=f32) - 99.0, (128, 16))
    shared["iotat"] = iotat
    return shared


def kernel(L0_attn_x, **weights):
    cc = np.ascontiguousarray
    x = cc(np.asarray(L0_attn_x, np.float32))
    shared = host_shared(**weights)
    nc = _get_nc(STAGE)
    in_maps = [dict(shared, x=cc(x[c])) for c in range(8)]
    res = run_bass_kernel_spmd(nc, in_maps, core_ids=list(range(8)))
    return np.stack([r["out"] for r in res.results], axis=0)


# version 05
# speedup vs baseline: 1.1512x; 1.1264x over previous
import math
import numpy as np
import concourse.bass as bass
import concourse.mybir as mybir
from concourse.bass_utils import run_bass_kernel_spmd
from contextlib import ExitStack

F32 = mybir.dt.float32
BF16 = mybir.dt.bfloat16
AF = mybir.ActivationFunctionType
ALU = mybir.AluOpType
AX = mybir.AxisListType

NT = 2048
NTC = 16
H = 1024
NK = 8
NE = 16
IM = 512
NHEAD = 16
HD = 64
STAGE = 3
DBG = 0


class Sched:

    def __init__(self):
        self.ops = []
        self.last_w = {}
        self.readers = {}
        self.drain_mode = False
        self._last_was_drain = {}

    def add(self, eng, fn, reads=(), writes=(), dma=None):
        if (self.drain_mode and fn is not None and dma is None
                and eng in ("vector", "scalar")
                and not self._last_was_drain.get(eng, False)):
            self._last_was_drain[eng] = True
            dr = lambda h: h.drain()
            dr._is_drain = True
            self.add(eng, dr)
        self._last_was_drain[eng] = (fn is not None and dma is None
                                     and getattr(fn, "_is_drain", False))
        idx = len(self.ops)
        deps = set()
        for t in reads:
            if t in self.last_w:
                deps.add(self.last_w[t])
        for t in writes:
            if t in self.last_w:
                deps.add(self.last_w[t])
            for r in self.readers.get(t, ()):
                deps.add(r)
        self.ops.append(dict(idx=idx, eng=eng, fn=fn, deps=deps, dma=dma,
                             inc=None, dma_idx=None, waits=None))
        for t in reads:
            self.readers.setdefault(t, []).append(idx)
        for t in writes:
            self.last_w[t] = idx
            self.readers[t] = []
        return idx

    def emit(self, nc, ctx):
        ops = self.ops
        qcnt = {}
        for op in ops:
            if op["dma"]:
                q = op["dma"]
                op["dma_idx"] = qcnt.get(q, 0)
                qcnt[q] = op["dma_idx"] + 1
        need_inc = set()
        for op in ops:
            for d in op["deps"]:
                dop = ops[d]
                if dop["dma"]:
                    continue
                if dop["eng"] != op["eng"]:
                    need_inc.add(d)
        ectr = {}
        for op in ops:
            if op["idx"] in need_inc and not op["dma"]:
                ectr[op["eng"]] = ectr.get(op["eng"], 0) + 1
                op["inc"] = ectr[op["eng"]]
        sems = {}
        for e in sorted(set(o["eng"] for o in ops if o["inc"])):
            sems["E_" + e] = ctx.enter_context(nc.semaphore("sem_" + e))
        for q in sorted(qcnt):
            sems["Q_" + q] = ctx.enter_context(nc.semaphore("dma_" + q))
        hw = {}
        for op in ops:
            waits = {}
            for d in op["deps"]:
                dop = ops[d]
                if dop["dma"]:
                    key, val = "Q_" + dop["dma"], 16 * (dop["dma_idx"] + 1)
                elif dop["eng"] == op["eng"]:
                    continue
                else:
                    key, val = "E_" + dop["eng"], dop["inc"]
                    assert val is not None
                if waits.get(key, 0) < val:
                    waits[key] = val
            wl = []
            for key, val in sorted(waits.items()):
                k2 = (op["eng"], key)
                if hw.get(k2, 0) >= val:
                    continue
                hw[k2] = val
                wl.append((key, val))
            op["waits"] = wl
        block = ctx.enter_context(nc.Block())
        for eng in ("sync", "gpsimd", "tensor", "vector", "scalar"):
            eng_ops = [o for o in ops if o["eng"] == eng]
            if not eng_ops:
                continue

            def body(h, eng_ops=eng_ops):
                for o in eng_ops:
                    for key, val in o["waits"]:
                        h.wait_ge(sems[key], val)
                    if o["fn"] is None:
                        continue
                    inst = o["fn"](h)
                    if o["dma"]:
                        inst.then_inc(sems["Q_" + o["dma"]], 16)
                    elif o["inc"]:
                        inst.then_inc(sems["E_" + o["eng"]], 1)

            getattr(block, eng)(body)


def build_nc(stage=STAGE):
    nc = bass.Bass("TRN2", target_bir_lowering=False)
    dt = {}
    def din(name, shape, dtype=F32):
        dt[name] = nc.dram_tensor(name, shape, dtype, kind="ExternalInput").ap()
        return dt[name]

    X = din("x", [NT, H])
    QKVWT = din("qkvwt", [H, 3 * H])
    QKVB24 = din("qkvb24", [24, 128])
    QKVBR = din("qkvbr", [1, H])
    OUTWT = din("outwt", [H, H])
    OUTBR = din("outbr", [1, H])
    RWT = din("rwt", [H, NE])
    EB = din("eb", [128, 256])
    IOTAT = din("iotat", [128, 320])
    GWT = din("gwt", [NE, H, IM])
    UWT = din("uwt", [NE, H, IM])
    DWT = din("dwt", [NE, IM, H])
    SHGT = din("shgt", [H, H])
    SHUT = din("shut", [H, H])
    SHDT = din("shdt", [H, H])
    IDENT = din("ident", [128, 128])
    CONSTF = din("constf", [128, 18])
    ONES = din("ones", [128, 128])
    OUT = nc.dram_tensor("out", [NT, H], F32, kind="ExternalOutput").ap()
    if DBG:
        DBGS = nc.dram_tensor("dbg_scores", [128, 256], F32, kind="ExternalOutput").ap()
        DBGC = nc.dram_tensor("dbg_comb", [128, 256], F32, kind="ExternalOutput").ap()
        DBGX = nc.dram_tensor("dbg_xlt", [128, NK * NT], BF16, kind="ExternalOutput").ap()
        DBGOT = nc.dram_tensor("dbg_ot", [128, 16384], BF16, kind="ExternalOutput").ap()
        DBGQK = nc.dram_tensor("dbg_qk", [128, 8192], BF16, kind="ExternalOutput").ap()
        DBGVV = nc.dram_tensor("dbg_vv", [128, NTC * 130], BF16, kind="ExternalOutput").ap()

    with ExitStack() as ctx:
        def sb(name, shape, dtp):
            return ctx.enter_context(nc.sbuf_tensor(name, shape, dtp)).ap()
        def psum(name, shape):
            return ctx.enter_context(nc.psum_tensor(name, shape, F32)).ap()

        x_sb = sb("x_sb", [128, NTC * H], F32)
        xlt = sb("xlt", [128, NK * NT], BF16)
        big1 = sb("big1", [128, 16384], BF16)
        r1 = sb("r1", [128, 8192], BF16)
        r2 = sb("r2", [128, 4096], BF16)
        vv = sb("vv", [128, NTC * 130], BF16)
        sc2k = sb("sc2k", [128, 2048], BF16)
        xscr = sc2k.bitcast(F32)
        rcbc = sb("rcbc", [128, 1024], BF16)
        scores = sb("scores", [128, 256], F32)
        comb = sb("comb", [128, 256], F32)
        rtb = sb("rtb", [128, 1536], F32)
        ln_sb = sb("ln_sb", [128, 8], F32)
        bias_cols = sb("bias_cols", [128, 24], F32)
        bqs = sb("bqs", [128, 8], F32)
        qkvb24s = sb("qkvb24s", [24, 128], F32)
        qkvbr_bf = sb("qkvbr_bf", [1, H], BF16)
        outbr_bf = sb("outbr_bf", [1, H], BF16)
        rw_sb = sb("rw_sb", [128, 128], BF16)
        ones_bf = sb("ones_bf", [128, 128], BF16)
        ident_sb = sb("ident_sb", [128, 128], F32)
        constf = sb("constf_sb", [128, 18], F32)
        eb_sb = sb("eb_sb", [128, 256], F32)
        iotat = sb("iotat_sb", [128, 320], F32)

        PA = psum("PA", [128, 1024])
        PB = psum("PB", [128, 1024])
        PC = psum("PC", [128, 1024])
        PD = psum("PD", [128, 1024])

        S = Sched()

        iota99 = constf[:, 0:16]
        eps_col = constf[:, 16:17]
        iota99 = constf[:, 0:16]
        eps_col = constf[:, 16:17]

        def x_ap(tc):
            return x_sb[:, tc * H:(tc + 1) * H]

        def xlt_ap(k, t0, t1):
            return xlt[:, k * NT + t0: k * NT + t1]

        def he_ap(par, ic, t0, t1):
            return big1[:, par * 8192 + ic * NT + t0: par * 8192 + ic * NT + t1]

        def ot_ap(pr, t0, t1, plo=0, phi=128):
            return big1[plo:phi, pr * NT + t0: pr * NT + t1]

        def gw_ap(k, i0, i1):
            return r1[:, k * IM + i0: k * IM + i1]

        def uw_ap(k, i0, i1):
            return r1[:, 4096 + k * IM + i0: 4096 + k * IM + i1]

        def dw_ap(ic, h0, h1):
            return r2[:, ic * H + h0: ic * H + h1]

        def qk_ap(w, t0, t1, plo=0, phi=128):
            return r1[plo:phi, w * NT + t0: w * NT + t1]

        def wo_ap(dc, f0, f1):
            return r1[:, dc * H + f0: dc * H + f1]

        def wqkv_ap(k, w, d0, d1):
            return r2[:, k * 512 + w * 128 + d0: k * 512 + w * 128 + d1]

        def vv_ap(tc, c0, c1):
            return vv[:, tc * 130 + c0: tc * 130 + c1]

        S.add("sync", lambda h: h.dma_start(out=ident_sb, in_=IDENT),
              writes=[("ident",)], dma="sync")
        S.add("sync", lambda h: h.dma_start(out=constf, in_=CONSTF),
              writes=[("constf",)], dma="sync")
        S.add("sync", lambda h: h.dma_start(out=eb_sb, in_=EB),
              writes=[("eb",)], dma="sync")
        S.add("sync", lambda h: h.dma_start(out=iotat, in_=IOTAT),
              writes=[("iotat",)], dma="sync")
        S.add("sync", lambda h: h.dma_start(out=qkvb24s, in_=QKVB24),
              writes=[("qkvb24",)], dma="sync")
        S.add("gpsimd", lambda h: h.dma_start(out=ones_bf, in_=ONES),
              writes=[("ones",)], dma="gpsimd")
        S.add("gpsimd", lambda h: h.dma_start(out=rw_sb[:, 0:128],
              in_=RWT.rearrange("(k p) e -> p k e", p=128)),
              writes=[("rw",)], dma="gpsimd")
        S.add("gpsimd", lambda h: h.dma_start(out=qkvbr_bf, in_=QKVBR),
              writes=[("qkvbr",)], dma="gpsimd")
        S.add("gpsimd", lambda h: h.dma_start(out=outbr_bf, in_=OUTBR),
              writes=[("outbr",)], dma="gpsimd")
        Xr = X.rearrange("(c p) h -> p c h", p=128)
        for g in range(4):
            S.add("sync", lambda h, g=g: h.dma_start(
                out=x_sb[:, g * 4 * H:(g + 1) * 4 * H],
                in_=Xr[:, 4 * g:4 * g + 4, :]),
                writes=[("x", 4 * g + j) for j in range(4)], dma="sync")

        def emit_ln(nth):
            for tc in range(NTC):
                xt = x_ap(tc)
                s_col, q_col = ln_sb[:, 0:1], ln_sb[:, 1:2]
                m_col, msq = ln_sb[:, 2:3], ln_sb[:, 3:4]
                var, std = ln_sb[:, 4:5], ln_sb[:, 5:6]
                rstd, negmrs = ln_sb[:, 6:7], ln_sb[:, 7:8]
                S.add("vector", lambda h, xt=xt, o=s_col: h.reduce_sum(o, xt, axis=AX.X),
                      reads=[("x", tc)], writes=[("lns",)])
                S.add("scalar", lambda h, xt=xt, o=q_col: h.activation(
                    xscr, xt, AF.Square, accum_out=o),
                    reads=[("x", tc)], writes=[("sc2k", 0), ("sc2k", 1), ("lnq",)])
                S.drain_mode = True
                S.add("vector", lambda h, i=s_col, o=m_col: h.tensor_scalar(
                    o, i, 1.0 / H, None, ALU.mult),
                    reads=[("lns",)], writes=[("lnm",)])
                S.add("vector", lambda h, i=m_col, o=msq: h.tensor_tensor(
                    o, i, i, ALU.mult), reads=[("lnm",)], writes=[("lnmsq",)])
                S.add("vector", lambda h, i=q_col, j=msq, o=var: h.scalar_tensor_tensor(
                    o, i, 1.0 / H, j, ALU.mult, ALU.subtract),
                    reads=[("lnq",), ("lnmsq",)], writes=[("lnvar",)])
                S.add("scalar", lambda h, i=var, o=std: h.activation(
                    o, i, AF.Sqrt, bias=eps_col),
                    reads=[("lnvar",), ("constf",)], writes=[("lnstd",)])
                S.add("vector", lambda h, i=std, o=rstd: h.reciprocal(o, i),
                      reads=[("lnstd",)], writes=[("lnrstd",)])
                S.add("vector", lambda h, i=m_col, j=rstd, o=negmrs:
                      h.scalar_tensor_tensor(o, i, -1.0, j, ALU.mult, ALU.mult),
                      reads=[("lnm",), ("lnrstd",)], writes=[("lnneg",)])
                S.add("scalar", lambda h, xt=xt, sc=rstd, b=negmrs: h.activation(
                    xscr, xt, AF.Identity, bias=b, scale=sc),
                    reads=[("x", tc), ("lnrstd",), ("lnneg",)], writes=[("sc2k", 0), ("sc2k", 1)])
                S.drain_mode = False
                for k in range(NK):
                    S.add("tensor", lambda h, k=k: h.transpose(
                        PD[:, k * 128:(k + 1) * 128],
                        xscr[:, k * 128:(k + 1) * 128], ident_sb),
                        reads=[("sc2k", 0), ("sc2k", 1), ("ident",)], writes=[("PD", k // 4)])
                for hh in range(2):
                    dst = xlt.rearrange("p (k t) -> p k t", k=NK)[
                        :, hh * 4:(hh + 1) * 4, tc * 128:(tc + 1) * 128]
                    S.add("scalar", lambda h, dst=dst, hh=hh: h.activation(
                        dst, PD[:, hh * 512:(hh + 1) * 512], AF.Identity),
                        reads=[("PD", hh)], writes=[("xlt",)])

        def emit_router_mms(call):
            for tc in range(NTC):
                slot = tc % 2
                po = PD[:, slot * 512: slot * 512 + 16]
                for k in range(NK):
                    S.add("tensor", lambda h, k=k, tc=tc, po=po: h.matmul(
                        po, xlt_ap(k, tc * 128, (tc + 1) * 128),
                        rw_sb[:, k * 16:(k + 1) * 16],
                        start=(k == 0), stop=(k == NK - 1)),
                        reads=[("rw",), ("xlt",)], writes=[("PD", slot)])
                S.add("scalar", lambda h, tc=tc, po=po: h.activation(
                    scores[:, tc * 16:(tc + 1) * 16], po, AF.Sigmoid),
                    reads=[("PD", tc % 2)], writes=[("scores", tc)])
        def emit_routing(call):
            rtb = vv.bitcast(F32)
            rtc = rcbc.bitcast(F32)
            SC256 = rtb[:, 0:256]
            MSK256 = rtb[:, 256:512]
            OHS256 = rtb[:, 512:768]
            EQ256 = rtb[:, 768:1024]
            GS64 = rtc[:, 0:64]
            T64a = rtc[:, 64:128]
            T64b = rtc[:, 128:192]
            M16 = rtc[:, 192:208]
            I16 = rtc[:, 208:224]
            D16 = rtc[:, 224:240]
            JK = rtc[:, 240:241]
            iota4t = iotat[:, 0:64]
            iota16t = iotat[:, 64:320]
            S.drain_mode = True
            V = lambda fn, reads=(), writes=(): S.add(
                "vector", fn, list(reads) + [("rtb",)], list(writes) + [("rtb",)])
            sc_reads = [("scores", tc) for tc in range(NTC)]
            V(lambda h: h.tensor_tensor(SC256, scores, eb_sb, ALU.add),
              reads=sc_reads + [("eb",)],
              writes=[("vv", t) for t in range(NTC)] + [("rcbc",)])
            sc4 = SC256.rearrange("p (c g e) -> p c g e", c=16, g=4)
            g3 = GS64.rearrange("p (c g) -> p c g", c=16)
            t3a = T64a.rearrange("p (c g) -> p c g", c=16)
            PAIRS = [(0, 1), (0, 2), (0, 3), (1, 2), (1, 3), (2, 3)]
            for j, (a, b) in enumerate(PAIRS):
                if j == 0:
                    V(lambda h, a=a, b=b: h.tensor_tensor(
                        g3, sc4[:, :, :, a], sc4[:, :, :, b], ALU.add))
                else:
                    V(lambda h, a=a, b=b: h.tensor_tensor(
                        t3a, sc4[:, :, :, a], sc4[:, :, :, b], ALU.add))
                    V(lambda h: h.tensor_tensor(GS64, GS64, T64a, ALU.max))
            i4t3 = iota4t.rearrange("p (c g) -> p c g", c=16)
            m16b4 = M16.unsqueeze(2).broadcast_to([128, 16, 4])
            i16b4 = I16.unsqueeze(2).broadcast_to([128, 16, 4])
            for grp, dst in ((1, T64a), (2, T64b)):
                d3 = dst.rearrange("p (c g) -> p c g", c=16)
                V(lambda h: h.tensor_reduce(M16, g3, AX.X, ALU.max))
                V(lambda h: h.tensor_copy(JK, M16[:, 0:1]))
                V(lambda h, d3=d3: h.tensor_tensor(d3, g3, m16b4, ALU.is_equal))
                V(lambda h, dst=dst: h.tensor_tensor(dst, dst, iota4t, ALU.mult))
                V(lambda h, d3=d3: h.tensor_reduce(I16, d3, AX.X, ALU.min))
                V(lambda h: h.tensor_copy(JK, I16[:, 0:1]))
                V(lambda h, d3=d3: h.tensor_tensor(d3, i4t3, i16b4, ALU.is_equal))
                if grp == 1:
                    V(lambda h: h.scalar_tensor_tensor(
                        GS64, T64a, -1e9, GS64, ALU.mult, ALU.add))
            V(lambda h: h.tensor_tensor(T64a, T64a, T64b, ALU.add))
            sm4 = MSK256.rearrange("p (c g e) -> p c g e", c=16, g=4)
            for j in range(4):
                V(lambda h, j=j: h.tensor_copy(sm4[:, :, :, j], t3a))
            V(lambda h: h.tensor_tensor(MSK256, SC256, MSK256, ALU.mult))
            m3 = MSK256.rearrange("p (c e) -> p c e", c=16)
            e3 = EQ256.rearrange("p (c e) -> p c e", c=16)
            a3 = SC256.rearrange("p (c e) -> p c e", c=16)
            i16t3 = iota16t.rearrange("p (c e) -> p c e", c=16)
            m16b16 = M16.unsqueeze(2).broadcast_to([128, 16, 16])
            i16b16 = I16.unsqueeze(2).broadcast_to([128, 16, 16])
            for it in range(4):
                V(lambda h: h.tensor_reduce(M16, m3, AX.X, ALU.max))
                V(lambda h: h.tensor_copy(JK, M16[:, 0:1]))
                V(lambda h: h.tensor_tensor(e3, m3, m16b16, ALU.is_equal))
                V(lambda h: h.tensor_tensor(SC256, EQ256, iota16t, ALU.mult))
                V(lambda h: h.tensor_reduce(I16, a3, AX.X, ALU.min))
                V(lambda h: h.tensor_copy(JK, I16[:, 0:1]))
                V(lambda h: h.tensor_tensor(e3, i16t3, i16b16, ALU.is_equal))
                if it == 0:
                    V(lambda h: h.tensor_copy(OHS256, EQ256))
                else:
                    V(lambda h: h.tensor_tensor(OHS256, OHS256, EQ256, ALU.add))
                V(lambda h: h.scalar_tensor_tensor(
                    MSK256, EQ256, -1e9, MSK256, ALU.mult, ALU.add))
            V(lambda h: h.tensor_tensor(EQ256, OHS256, scores, ALU.mult),
              reads=sc_reads)
            V(lambda h: h.tensor_reduce(D16, e3, AX.X, ALU.add))
            V(lambda h: h.tensor_scalar_add(D16, D16, 1e-20))
            V(lambda h: h.reciprocal(M16, D16))
            V(lambda h: h.tensor_copy(JK, M16[:, 0:1]))
            V(lambda h: h.scalar_tensor_tensor(
                comb.rearrange("p (c e) -> p c e", c=16), e3, 1.25, m16b16,
                ALU.mult, ALU.mult), writes=[("comb",)])
            V(lambda h: h.tensor_copy(JK, comb[:, 0:1]), reads=[("comb",)])
            S.drain_mode = False

        fus_ctr = [0]

        def gu_steps(ridx, e):
            par = ridx % 2
            steps = []
            for ic in range(4):
                for tb2 in range(2):
                    def step(ic=ic, tb2=tb2, par=par):
                        t0 = tb2 * 1024
                        for wname, wap, ps, pnm in (
                                ("gw", gw_ap, PA, "PA"),
                                ("uw", uw_ap, PB, "PB")):
                            wtok = (wname, ic // 2)
                            for hh in range(2):
                                for k in range(NK):
                                    S.add("tensor", lambda h, k=k, ic=ic, hh=hh,
                                          t0=t0, wap=wap, ps=ps: h.matmul(
                                              ps[:, hh * 512:(hh + 1) * 512],
                                              wap(k, ic * 128, (ic + 1) * 128),
                                              xlt[:, k * NT + t0 + hh * 512:
                                                  k * NT + t0 + hh * 512 + 512],
                                              start=(k == 0), stop=(k == NK - 1)),
                                          reads=[wtok, ("xlt",)],
                                          writes=[(pnm, hh)])
                        sp = fus_ctr[0] % 2
                        fus_ctr[0] += 1
                        s1 = sc2k[:, sp * 1024:(sp + 1) * 1024]
                        S.add("scalar", lambda h, s1=s1: h.activation(s1, PA, AF.Silu),
                              reads=[("PA", 0), ("PA", 1)],
                              writes=[("sc2k", 2 * sp), ("sc2k", 2 * sp + 1)])
                        S.add("vector", lambda h, s1=s1, par=par, ic=ic, t0=t0:
                              h.tensor_tensor(he_ap(par, ic, t0, t0 + 1024), s1, PB,
                                              ALU.mult),
                              reads=[("sc2k", 2 * sp), ("sc2k", 2 * sp + 1),
                                     ("PB", 0), ("PB", 1)], writes=[("B1", par)])
                    steps.append(step)
            return steps

        def down_steps(ridx, e):
            par = ridx % 2
            shared = e >= 16
            steps = []
            for tc in range(NTC):
                def step(tc=tc, par=par, shared=shared, e=e):
                    if tc % 2 == 0:
                        slot, stoks = PC, [("PC", 0), ("PC", 1)]
                    else:
                        slot, stoks = PD, [("PD", 0), ("PD", 1)]
                    for hh in range(2):
                        for ic in range(4):
                            S.add("tensor", lambda h, ic=ic, hh=hh, tc=tc, par=par,
                                  slot=slot: h.matmul(
                                      slot[:, hh * 512:(hh + 1) * 512],
                                      he_ap(par, ic, tc * 128, (tc + 1) * 128),
                                      dw_ap(ic, hh * 512, hh * 512 + 512),
                                      start=(ic == 0), stop=(ic == 3)),
                                  reads=[("B1", par), ("dw",)], writes=stoks)
                    if shared:
                        S.add("vector", lambda h, tc=tc, slot=slot:
                              h.scalar_tensor_tensor(x_ap(tc), slot, 0.5, x_ap(tc),
                                                     ALU.mult, ALU.add),
                              reads=stoks + [("x", tc)], writes=[("x", tc)])
                    else:
                        ccol = comb[:, tc * 16 + e: tc * 16 + e + 1]
                        S.add("vector", lambda h, tc=tc, slot=slot, ccol=ccol:
                              h.scalar_tensor_tensor(x_ap(tc), slot, ccol, x_ap(tc),
                                                     ALU.mult, ALU.add),
                              reads=stoks + [("x", tc), ("comb",)],
                              writes=[("x", tc)])
                steps.append(step)
            return steps

        def emit_ffn(call):
            emit_router_mms(call)
            rounds = [16, 17] + list(range(NE))
            srcs = []
            for e in rounds:
                if e >= 16:
                    j = e - 16
                    srcs.append((SHGT[:, j * IM:(j + 1) * IM],
                                 SHUT[:, j * IM:(j + 1) * IM],
                                 SHDT[j * IM:(j + 1) * IM, :]))
                else:
                    srcs.append((GWT[e], UWT[e], DWT[e]))
            r1v = r1.rearrange("p (w k i) -> p w k i", w=2, k=8)

            def gu_dmas(i):
                gsrc, usrc, _ = srcs[i]
                for hf in range(2):
                    S.add("gpsimd", lambda h, s=gsrc, hf=hf: h.dma_start(
                        out=r1v[:, 0, :, hf * 256:(hf + 1) * 256],
                        in_=s[:, hf * 256:(hf + 1) * 256].rearrange(
                            "(k p) i -> p k i", p=128)),
                        writes=[("gw", hf)], dma="gpsimd")
                    S.add("gpsimd", lambda h, s=usrc, hf=hf: h.dma_start(
                        out=r1v[:, 1, :, hf * 256:(hf + 1) * 256],
                        in_=s[:, hf * 256:(hf + 1) * 256].rearrange(
                            "(k p) i -> p k i", p=128)),
                        writes=[("uw", hf)], dma="gpsimd")

            gu_dmas(0)
            pend = []
            for i, e in enumerate(rounds):
                dsrc = srcs[i][2]
                gus = gu_steps(i, e)
                for j, g in enumerate(gus):
                    g()
                    for dn in pend[2 * j:2 * j + 2]:
                        dn()
                if i == 0:
                    emit_routing(call)
                if i + 1 < len(rounds):
                    gu_dmas(i + 1)
                S.add("gpsimd", lambda h, s=dsrc: h.dma_start(
                    out=r2, in_=s.rearrange("(ic p) h -> p ic h", p=128)),
                    writes=[("dw",)], dma="gpsimd")
                pend = down_steps(i, e)
            for dn in pend:
                dn()

        def emit_attention():
            S.add("vector", lambda h: h.memset(
                vv.rearrange("p (tc c) -> p tc c", tc=NTC)[:, :, 64:65], 1.0),
                writes=[("vvones",), ("rtb",)])
            S.add("vector", lambda h: h.memset(
                vv.rearrange("p (tc c) -> p tc c", tc=NTC)[:, :, 129:130], 1.0),
                writes=[("vvones2",), ("rtb",)])
            S.add("tensor", lambda h: h.transpose(
                PD[:, 0:24], qkvb24s, ident_sb[0:24, 0:24]),
                reads=[("qkvb24",), ("ident",)], writes=[("PD", 0)])
            S.add("vector", lambda h: h.tensor_copy(bias_cols, PD[:, 0:24]),
                  reads=[("PD", 0)], writes=[("bcols",)])
            S.drain_mode = True
            S.add("vector", lambda h: h.tensor_scalar(
                bqs, bias_cols[:, 0:8], 0.125, None, ALU.mult),
                reads=[("bcols",)], writes=[("bqs",)])
            S.drain_mode = False
            def wqkv_dma(pr):
                for w, c0 in ((0, pr * 128), (1, H + pr * 128), (2, 2 * H + pr * 128)):
                    src = QKVWT[:, c0:c0 + 128].rearrange("(k p) d -> p k d", p=128)
                    dst = r2.rearrange("p (k m) -> p k m", k=8)[:, :, w * 128:(w + 1) * 128]
                    S.add("gpsimd", lambda h, src=src, dst=dst: h.dma_start(
                        out=dst, in_=src), writes=[("dw",)], dma="gpsimd")
            wqkv_dma(0)
            for pr in range(8):
                if pr + 1 < 8:
                    pass
                slots = [(PA, ("PA", 0), 0), (PA, ("PA", 1), 512), (PB, ("PB", 0), 0),
                         (PB, ("PB", 1), 512)]
                for w, name in ((0, "q"), (1, "k")):
                    for tt in range(4):
                        ps, ptok, off = slots[tt]
                        for k in range(NK):
                            S.add("tensor", lambda h, k=k, w=w, tt=tt, ps=ps, off=off:
                                  h.matmul(ps[:, off:off + 512],
                                           wqkv_ap(k, w, 0, 128),
                                           xlt_ap(k, tt * 512, (tt + 1) * 512),
                                           start=(k == 0), stop=(k == NK - 1)),
                                  reads=[("dw",), ("xlt",)], writes=[ptok])
                        if w == 0:
                            bc = bqs[:, pr:pr + 1]
                            scl = 0.125
                            btok = ("bqs",)
                        else:
                            bc = bias_cols[:, 8 + pr:9 + pr]
                            scl = 1.0
                            btok = ("bcols",)
                        S.add("scalar", lambda h, w=w, tt=tt, ps=ps, off=off, bc=bc,
                              scl=scl: h.activation(
                                  qk_ap(w, tt * 512, (tt + 1) * 512), ps[:, off:off + 512],
                                  AF.Identity, bias=bc, scale=scl),
                              reads=[ptok, btok], writes=[("gw", 0), ("gw", 1)])
                for tc in range(NTC):
                    po = PD[:, (tc % 2) * 512:(tc % 2) * 512 + 128]
                    ptok = ("PD", tc % 2)
                    for k in range(NK):
                        S.add("tensor", lambda h, k=k, tc=tc, po=po: h.matmul(
                            po, xlt_ap(k, tc * 128, (tc + 1) * 128),
                            wqkv_ap(k, 2, 0, 128),
                            start=(k == 0), stop=False),
                            reads=[("dw",), ("xlt",)], writes=[ptok])
                    S.add("tensor", lambda h, tc=tc, po=po, pr=pr: h.matmul(
                        po, ones_bf[0:1, 0:128],
                        qkvbr_bf[0:1, pr * 128: pr * 128 + 128],
                        start=False, stop=True),
                        reads=[("ones",), ("qkvbr",)], writes=[ptok])
                    dst = vv.rearrange("p (tc b c) -> p tc b c", tc=NTC, b=2)[
                        :, tc, :, 0:64]
                    S.add("scalar", lambda h, dst=dst, po=po: h.activation(
                        dst, po, AF.Identity),
                        reads=[ptok], writes=[("vv", tc), ("rtb",)])
                if pr + 1 < 8:
                    wqkv_dma(pr + 1)
                if DBG >= 2 and pr < 7:
                    continue
                if DBG >= 3:
                    continue
                SSL = [(PA, 0, ("PA", 0)), (PA, 512, ("PA", 1)),
                       (PB, 0, ("PB", 0)), (PB, 512, ("PB", 1)),
                       (PD, 512, ("PD", 1))]
                for it in range(4):
                    i0 = it * 512
                    for jg in range(NTC):
                        sa, oa, ta = SSL[(2 * jg) % 5]
                        sb, ob, tb = SSL[(2 * jg + 1) % 5]
                        S.add("tensor", lambda h, jg=jg, i0=i0, sa=sa, oa=oa: h.matmul(
                            sa[:, oa:oa + 512],
                            qk_ap(1, jg * 128, (jg + 1) * 128, 0, 64),
                            qk_ap(0, i0, i0 + 512, 0, 64),
                            start=True, stop=True),
                            reads=[("gw", 0), ("gw", 1)], writes=[ta])
                        S.add("tensor", lambda h, jg=jg, i0=i0, sb=sb, ob=ob: h.matmul(
                            sb[:, ob:ob + 512],
                            qk_ap(1, jg * 128, (jg + 1) * 128, 64, 128),
                            qk_ap(0, i0, i0 + 512, 64, 128),
                            start=True, stop=True),
                            reads=[("gw", 0), ("gw", 1)], writes=[tb])
                        qa = (2 * jg) % 4
                        qb = (2 * jg + 1) % 4
                        pta = sc2k[:, qa * 512:(qa + 1) * 512]
                        ptb = sc2k[:, qb * 512:(qb + 1) * 512]
                        S.add("scalar", lambda h, pta=pta, sa=sa, oa=oa: h.activation(
                            pta, sa[:, oa:oa + 512], AF.Exp),
                            reads=[ta], writes=[("sc2k", qa)])
                        S.add("scalar", lambda h, ptb=ptb, sb=sb, ob=ob: h.activation(
                            ptb, sb[:, ob:ob + 512], AF.Exp),
                            reads=[tb], writes=[("sc2k", qb)])
                        S.add("tensor", lambda h, jg=jg, pta=pta: h.matmul(
                            PC[0:65, 0:512], vv_ap(jg, 0, 65), pta,
                            start=(jg == 0), stop=(jg == NTC - 1)),
                            reads=[("sc2k", qa), ("vv", jg), ("vvones",)],
                            writes=[("PC", 0)])
                        S.add("tensor", lambda h, jg=jg, ptb=ptb: h.matmul(
                            PC[64:128, 512:1024], vv_ap(jg, 65, 129), ptb,
                            tile_position=(0, 64),
                            start=(jg == 0), stop=(jg == NTC - 1)),
                            reads=[("sc2k", qb), ("vv", jg)], writes=[("PC", 1)])
                        S.add("tensor", lambda h, jg=jg, ptb=ptb: h.matmul(
                            PD[0:1, 0:512], ones_bf[:, 0:1], ptb,
                            start=(jg == 0), stop=(jg == NTC - 1)),
                            reads=[("sc2k", qb), ("ones",)], writes=[("PD", 0)])
                    S.drain_mode = True
                    def _recip(h, o, i):
                        with nc.allow_low_precision(reason="softmax recip feeds bf16 bcast matmul"):
                            return h.reciprocal(o, i)
                    S.add("vector", lambda h: _recip(h,
                        rcbc[64:65, 0:512], PC[64:65, 0:512]),
                        reads=[("PC", 0)], writes=[("rcbc",), ("rtb",)])
                    S.add("vector", lambda h: _recip(h,
                        rcbc[0:1, 512:1024], PD[0:1, 0:512]),
                        reads=[("PD", 0)], writes=[("rcbc",), ("rtb",)])
                    S.add("tensor", lambda h: h.matmul(
                        PA[:, 0:512], ones_bf[64:65, 0:128], rcbc[64:65, 0:512],
                        start=True, stop=True),
                        reads=[("rcbc",), ("ones",)], writes=[("PA", 0)])
                    S.add("tensor", lambda h: h.matmul(
                        PA[:, 512:1024], ones_bf[0:1, 0:128], rcbc[0:1, 512:1024],
                        start=True, stop=True),
                        reads=[("rcbc",), ("ones",)], writes=[("PA", 1)])
                    S.add("vector", lambda h: h.tensor_copy(rcbc, PA),
                          reads=[("PA", 0), ("PA", 1)], writes=[("rcbc",), ("rtb",)])
                    S.add("vector", lambda h, pr=pr, i0=i0: h.tensor_tensor(
                        ot_ap(pr, i0, i0 + 512, 0, 64), PC[0:64, 0:512],
                        rcbc[0:64, 0:512], ALU.mult),
                        reads=[("PC", 0), ("rcbc",)], writes=[("B1", pr // 4)])
                    S.add("vector", lambda h, pr=pr, i0=i0: h.tensor_tensor(
                        ot_ap(pr, i0, i0 + 512, 64, 128), PC[64:128, 512:1024],
                        rcbc[64:128, 512:1024], ALU.mult),
                        reads=[("PC", 1), ("rcbc",)], writes=[("B1", pr // 4)])
                    S.drain_mode = False
            if DBG >= 2:
                return
            S.add("gpsimd", lambda h: h.dma_start(
                out=r1, in_=OUTWT.rearrange("(dc p) f -> p dc f", p=128)),
                writes=[("gw", 0), ("gw", 1), ("uw", 0), ("uw", 1)], dma="gpsimd")
            for tc in range(NTC):
                ps, ptoks = (PA, [("PA", 0), ("PA", 1)]) if tc % 2 == 0 else (PB, [("PB", 0), ("PB", 1)])
                for hh in range(2):
                    for dc in range(NK):
                        S.add("tensor", lambda h, dc=dc, hh=hh, tc=tc, ps=ps: h.matmul(
                            ps[:, hh * 512:(hh + 1) * 512],
                            ot_ap(dc, tc * 128, (tc + 1) * 128),
                            wo_ap(dc, hh * 512, hh * 512 + 512),
                            start=(dc == 0), stop=False),
                            reads=[("B1", dc // 4), ("gw", 0), ("gw", 1), ("uw", 0), ("uw", 1)], writes=ptoks)
                    S.add("tensor", lambda h, hh=hh, tc=tc, ps=ps: h.matmul(
                        ps[:, hh * 512:(hh + 1) * 512], ones_bf[0:1, 0:128],
                        outbr_bf[0:1, hh * 512: hh * 512 + 512],
                        start=False, stop=True),
                        reads=[("ones",), ("outbr",)], writes=ptoks)
                S.add("vector", lambda h, tc=tc, ps=ps: h.tensor_tensor(
                    x_ap(tc), x_ap(tc), ps, ALU.add),
                    reads=ptoks + [("x", tc)], writes=[("x", tc)])

        if stage >= 1:
            emit_ln(0)
            emit_ffn(0)
        if stage >= 2:
            emit_ln(1)
            emit_attention()
        if stage >= 3:
            emit_ln(2)
            emit_ffn(1)

        if DBG:
            S.add("sync", lambda h: h.dma_start(out=DBGS, in_=scores),
                  reads=[("scores", tc) for tc in range(NTC)], writes=[("OUT",)], dma="sync")
            S.add("sync", lambda h: h.dma_start(out=DBGC, in_=comb),
                  reads=[("comb", tc) for tc in range(NTC)], writes=[("OUT",)], dma="sync")
            S.add("sync", lambda h: h.dma_start(out=DBGX, in_=xlt),
                  reads=[("xlt",)], writes=[("OUT",)], dma="sync")
            if stage >= 2:
                S.add("sync", lambda h: h.dma_start(out=DBGOT, in_=big1),
                      reads=[("B1", 0), ("B1", 1)], writes=[("OUT",)], dma="sync")
                S.add("sync", lambda h: h.dma_start(out=DBGQK, in_=r1),
                      reads=[("gw", 0), ("gw", 1), ("uw", 0), ("uw", 1)], writes=[("OUT",)], dma="sync")
                S.add("sync", lambda h: h.dma_start(out=DBGVV, in_=vv),
                      reads=[("vv", tc) for tc in range(NTC)], writes=[("OUT",)], dma="sync")
        OUTr = OUT.rearrange("(c p) h -> p c h", p=128)
        for g in range(4):
            S.add("sync", lambda h, g=g: h.dma_start(
                out=OUTr[:, 4 * g:4 * g + 4, :],
                in_=x_sb[:, g * 4 * H:(g + 1) * 4 * H]),
                reads=[("x", 4 * g + j) for j in range(4)],
                writes=[("OUT",)], dma="sync")
        S.add("sync", None, reads=[("OUT",)])

        S.emit(nc, ctx)
    return nc


_NC_CACHE = {}


def _get_nc(stage):
    if stage not in _NC_CACHE:
        _NC_CACHE[stage] = build_nc(stage)
    return _NC_CACHE[stage]


def host_shared(L0_attn_qkv_w, L0_attn_qkv_b, L0_attn_out_w, L0_attn_out_b,
                L0_ffn_router_weight, L0_ffn_e_bias, L0_ffn_gate_w, L0_ffn_up_w,
                L0_ffn_down_w, L0_ffn_sh_gate_w, L0_ffn_sh_up_w, L0_ffn_sh_down_w,
                **_ignored):
    f32 = np.float32
    cc = np.ascontiguousarray
    shared = {
        "qkvwt": cc(np.asarray(L0_attn_qkv_w, f32).T),
        "qkvb24": cc(np.asarray(L0_attn_qkv_b, f32).reshape(24, 128)),
        "qkvbr": cc(np.asarray(L0_attn_qkv_b, f32).reshape(1, 3 * H)[:, 2 * H:]),
        "outwt": cc(np.asarray(L0_attn_out_w, f32).T),
        "outbr": cc(np.asarray(L0_attn_out_b, f32).reshape(1, H)),
        "rwt": cc(np.asarray(L0_ffn_router_weight, f32).T),
        "eb": cc(np.tile(np.asarray(L0_ffn_e_bias, f32).reshape(1, NE), (128, 16))),
        "gwt": cc(np.asarray(L0_ffn_gate_w, f32).transpose(0, 2, 1)),
        "uwt": cc(np.asarray(L0_ffn_up_w, f32).transpose(0, 2, 1)),
        "dwt": cc(np.asarray(L0_ffn_down_w, f32).transpose(0, 2, 1)),
        "shgt": cc(np.asarray(L0_ffn_sh_gate_w, f32).T),
        "shut": cc(np.asarray(L0_ffn_sh_up_w, f32).T),
        "shdt": cc(np.asarray(L0_ffn_sh_down_w, f32).T),
        "ident": np.eye(128, dtype=f32),
        "ones": np.ones((128, 128), f32),
    }
    constf = np.zeros((128, 18), f32)
    constf[:, 0:16] = np.arange(16, dtype=f32) - 99.0
    constf[:, 16] = 1e-5
    shared["constf"] = constf
    iotat = np.zeros((128, 320), f32)
    iotat[:, 0:64] = np.tile(np.arange(4, dtype=f32) - 99.0, (128, 16))
    iotat[:, 64:320] = np.tile(np.arange(16, dtype=f32) - 99.0, (128, 16))
    shared["iotat"] = iotat
    return shared


def kernel(L0_attn_x, **weights):
    cc = np.ascontiguousarray
    x = cc(np.asarray(L0_attn_x, np.float32))
    shared = host_shared(**weights)
    nc = _get_nc(STAGE)
    in_maps = [dict(shared, x=cc(x[c])) for c in range(8)]
    res = run_bass_kernel_spmd(nc, in_maps, core_ids=list(range(8)))
    return np.stack([r["out"] for r in res.results], axis=0)
